# version 10
# baseline (speedup 1.0000x reference)
import numpy as np
import ml_dtypes
import concourse.bass as bass
import concourse.mybir as mybir
from concourse.bass_utils import run_bass_kernel_spmd

F32 = mybir.dt.float32
BF16 = mybir.dt.bfloat16
I32 = mybir.dt.int32
AF = mybir.ActivationFunctionType
ALU = mybir.AluOpType
AX = mybir.AxisListType

R = 8
T = 1024
S = 8192
D = 2048
DH = 128
NT = 8
NPIECE = 4
NPASS = 4
PIECE_ROWS = 1536
BLK_ROWS = NPIECE * PIECE_ROWS
UPP = 32
NGROUP = 8
STORES_PER_PASS = 48


def build_nc(stage="full"):
    LVL = {"kv": 1, "sm": 2, "sm_nodq": 2, "attn": 3, "attn_nodq": 3,
           "full": 4}[stage]
    DQ = LVL >= 2 and not stage.endswith("nodq")
    FULL = LVL >= 4
    nc = bass.Bass("TRN2", target_bir_lowering=False, debug=False)

    x_in = nc.dram_tensor("x_in", [T, D], F32, kind="ExternalInput").ap()
    wk_in = nc.dram_tensor("wk_in", [D, DH], F32, kind="ExternalInput").ap()
    wv_in = nc.dram_tensor("wv_in", [D, DH], F32, kind="ExternalInput").ap()
    bkv_in = nc.dram_tensor("bkv_in", [1, 256], F32, kind="ExternalInput").ap()
    bo_in = nc.dram_tensor("bo_in", [1, D], F32, kind="ExternalInput").ap()
    wo_in = nc.dram_tensor("wo_in", [D, D], F32, kind="ExternalInput").ap()
    gpk_in = nc.dram_tensor("gpk_in", [1024, D], I32, kind="ExternalInput").ap()
    gsc_in = nc.dram_tensor("gsc_in", [64, D], F32, kind="ExternalInput").ap()
    upk_in = nc.dram_tensor("upk_in", [1024, D], I32, kind="ExternalInput").ap()
    usc_in = nc.dram_tensor("usc_in", [64, D], F32, kind="ExternalInput").ap()
    dpk_in = nc.dram_tensor("dpk_in", [1024, D], I32, kind="ExternalInput").ap()
    dsc_in = nc.dram_tensor("dsc_in", [64, D], F32, kind="ExternalInput").ap()
    out_y = nc.dram_tensor("out_y", [T, D], F32, kind="ExternalOutput").ap()

    kvg_i = nc.dram_tensor("kvg_i", [T, 272], BF16)
    kvg_o = nc.dram_tensor("kvg_o", [S, 272], BF16, addr_space="Shared")
    wrm_i = nc.dram_tensor("wrm_i", [1, 16], BF16)
    wrm_o = nc.dram_tensor("wrm_o", [8, 16], BF16, addr_space="Shared")
    gw_i = nc.dram_tensor("gw_i", [BLK_ROWS, 2048], BF16)
    gw_o = nc.dram_tensor("gw_o", [NPIECE * R * PIECE_ROWS, 2048], BF16, addr_space="Shared")
    gse_g = nc.dram_tensor("gse_g", [1024, 2048], F32)
    gse_u = nc.dram_tensor("gse_u", [1024, 2048], F32)
    gse_d = nc.dram_tensor("gse_d", [1024, 2048], F32)
    yb_e = nc.dram_tensor("yb_e", [T, 1024], BF16)
    yb_o = nc.dram_tensor("yb_o", [T, 1024], BF16)
    m_d = nc.dram_tensor("m_d", [1, 256], F32)

    gwi_v = gw_i.ap().rearrange("(pc ar ch w) c -> pc ar ch w c",
                                ar=3, ch=4, w=128)
    gwo_v = gw_o.ap().rearrange("(pc r ar ch w) c -> pc r ar ch w c",
                                r=R, ar=3, ch=4, w=128)

    from contextlib import ExitStack
    with ExitStack() as ES:
        e = ES.enter_context
        stream = e(nc.sbuf_tensor("stream", [128, NT * D], F32))
        pkb = [e(nc.sbuf_tensor(f"pkb{i}", [128, 512], I32)) for i in range(4)]
        scb = [e(nc.sbuf_tensor(f"scb{i}", [128, 512], F32)) for i in range(4)]
        dsc = {n: e(nc.sbuf_tensor(f"dq_{n}", [128, 1024], BF16))
               for n in ("mk", "u", "r4", "b")}
        dqo = [e(nc.sbuf_tensor(f"dqo{i}", [128, 1024], BF16)) for i in range(2)]
        msum = e(nc.sbuf_tensor("msum", [128, NT], F32))
        sscol = e(nc.sbuf_tensor("sscol", [128, NT], F32))
        nmean = e(nc.sbuf_tensor("nmean", [128, NT], F32))
        rstd = e(nc.sbuf_tensor("rstd", [128, NT], F32))
        vtm = e(nc.sbuf_tensor("vtm", [128, NT], F32))
        vtm2 = e(nc.sbuf_tensor("vtm2", [128, NT], F32))
        consts = e(nc.sbuf_tensor("consts", [128, 2], F32))
        ones_t = e(nc.sbuf_tensor("ones_t", [128, 128], BF16))
        idm = e(nc.sbuf_tensor("idm", [128, 128], BF16))
        P = [e(nc.psum_tensor(f"P{i}", [128, 512], F32)) for i in range(8)]

        names = ["q_ina0", "q_ina1", "q_ina2", "q_ina3", "q_ina4", "q_ina5", "q_ina6", "q_ina7", "q_inb", "q_wkv", "v_one", "p_id", "a_xc", "t_xp", "e_xp", "a_yc", "t_yp", "e_yp", "q_wo", "q_kvs", "q_kT",
                 "q_dqp0", "q_dqp1", "q_dqp2", "q_dqp3", "q_dqg0", "q_dqg1", "q_dqg2", "q_dqg3", "q_gwS0", "q_gwS1", "q_wc0", "q_wc1", "q_dw0", "q_dw1",
                 "q_yb", "q_yT", "q_out", "q_M", "q_gse0", "q_gse1", "q_gse2", "q_vT", "q_kr", "q_va",
                 "c_kv", "c_M",
                 "t_kv", "e_kv", "a_nrm", "v_M", "t_sT", "a_exp", "t_pv",
                 "v_ao", "t_pj", "v_pj", "a_ln", "v_ln", "v_ys",
                 "e_dq", "t_gu", "a_ge", "v_it", "t_dn", "v_da",
                 "a_ln2", "v_ln2"]
        sem = {n: e(nc.semaphore(n)) for n in names}
        c_gw = [e(nc.semaphore(f"c_gw{p}")) for p in range(NPIECE)]
        block = e(nc.Block())

        nc.const_aps.register(F32, 1e-5, consts[:, 0:1])

        def dq_units(d):
            us = [("g", d, wt, 0) for wt in range(8)]
            us += [("u", d, wt, 0) for wt in range(8)]
            us += [("d", d, wt, qh) for wt in range(4) for qh in range(4)]
            return us

        ALL_UNITS = [u for d in range(NPASS) for u in dq_units(d)]
        N_UNITS = len(ALL_UNITS) if FULL else UPP
        SP = [0]
        for _u in ALL_UNITS:
            SP.append(SP[-1] + (2 if _u[0] in ("g", "u") else 1))
        SPC = [[0], [0]]
        for _i, _u in enumerate(ALL_UNITS):
            ns = 2 if _u[0] in ("g", "u") else 1
            SPC[0].append(SPC[0][-1] + (ns if _i % 2 == 0 else 0))
            SPC[1].append(SPC[1][-1] + (ns if _i % 2 == 1 else 0))

        def gse_wait(eng, uidx):
            kind, d, wt, hh = ALL_UNITS[uidx]
            if kind in ("g", "u") and d == 0:
                eng.wait_ge(sem["q_gse0"], 32)
            elif kind == "d":
                eng.wait_ge(sem["q_gse1"], 16)
            else:
                eng.wait_ge(sem["q_gse2"], 32)

        def dq_issue_loads(eng, uidx):
            kind, d, wt, hh = ALL_UNITS[uidx]
            par = uidx % 4
            if uidx < 8:
                qd = sem[f"q_dqp{uidx % 4}"]
            else:
                qd = sem[f"q_dqg{uidx % 4}"]
            if uidx >= 3:
                eng.wait_ge(sem["e_dq"], uidx - 3)
            gse_wait(eng, uidx)
            if kind in ("g", "u"):
                pk_src = gpk_in if kind == "g" else upk_in
                sc_src = gse_g if kind == "g" else gse_u
                eng.dma_start(out=pkb[par][:, 0:512],
                              in_=pk_src[128 * wt:128 * (wt + 1), 512 * d:512 * d + 512]
                              ).then_inc(qd, 16)
                eng.dma_start(out=scb[par][:, 0:512],
                              in_=sc_src.ap()[128 * wt:128 * (wt + 1),
                                              512 * d:512 * d + 512]
                              ).then_inc(qd, 16)
            else:
                w0 = 512 * (d % 2) + 128 * wt
                eng.dma_start(out=pkb[par][:, 0:512],
                              in_=dpk_in[w0:w0 + 128, 512 * hh:512 * hh + 512]
                              ).then_inc(qd, 16)
                eng.dma_start(out=scb[par][:, 0:512],
                              in_=gse_d.ap()[w0:w0 + 128, 512 * hh:512 * hh + 512]
                              ).then_inc(qd, 16)

        def dq_decode(vec, uidx):
            kind, d, wt, hh = ALL_UNITS[uidx]
            par = uidx % 4
            pk = pkb[par][:, 0:512]
            sc = scb[par][:, 0:512]
            W = 1024 if kind in ("g", "u") else 512
            slot = dqo[uidx % 2]
            cat = slot[:, 0:W]
            if uidx < 8:
                vec.wait_ge(sem[f"q_dqp{uidx % 4}"], 32 * (uidx // 4 + 1))
            else:
                vec.wait_ge(sem[f"q_dqg{uidx % 4}"],
                            32 * ((uidx - 8) // 4 + 1))
            if uidx >= 2:
                vec.wait_ge(sem[f"q_gwS{uidx % 2}"], 16 * SPC[uidx % 2][uidx - 1])
            if kind == "d":
                vec.tensor_scalar(out=scb[par][:, 0:512], in0=sc, scalar1=0.5,
                                  scalar2=None, op0=ALU.mult)
                vec.drain()
            if kind in ("g", "u"):
                vec.tensor_single_scalar(out=dsc["u"].bitcast(I32)[:, 0:512], in_=pk, scalar=15,
                                         op=ALU.bitwise_and)
                vec.drain()
                vec.tensor_copy(out=slot[:, 0:512], in_=dsc["u"].bitcast(I32)[:, 0:512])
                vec.drain()
                vec.tensor_single_scalar(out=dsc["u"].bitcast(I32)[:, 0:512], in_=pk, scalar=4,
                                         op=ALU.logical_shift_right)
                vec.drain()
                vec.tensor_copy(out=slot[:, 512:1024], in_=dsc["u"].bitcast(I32)[:, 0:512])
                vec.drain()
            else:
                op = ALU.logical_shift_right if d >= 2 else ALU.bitwise_and
                s0 = 4 if d >= 2 else 15
                vec.tensor_single_scalar(out=dsc["u"].bitcast(I32)[:, 0:512], in_=pk, scalar=s0, op=op)
                vec.drain()
                vec.tensor_copy(out=slot[:, 0:512], in_=dsc["u"].bitcast(I32)[:, 0:512])
                vec.drain()
            mk, u, r4, bb = (dsc["mk"][:, 0:W], dsc["u"][:, 0:W],
                             dsc["r4"][:, 0:W], dsc["b"][:, 0:W])
            vec.tensor_scalar(out=mk, in0=cat, scalar1=7.5, scalar2=None,
                              op0=ALU.is_ge)
            vec.drain()
            vec.scalar_tensor_tensor(out=u, in0=mk, scalar=-8.0, in1=cat,
                                     op0=ALU.mult, op1=ALU.add)
            vec.drain()
            vec.tensor_scalar(out=r4, in0=u, scalar1=-4.0, scalar2=0.0,
                              op0=ALU.add, op1=ALU.max)
            vec.tensor_scalar(out=bb, in0=u, scalar1=-6.0, scalar2=0.0,
                              op0=ALU.add, op1=ALU.max)
            vec.tensor_scalar(out=cat, in0=mk, scalar1=-2.0, scalar2=1.0,
                              op0=ALU.mult, op1=ALU.add)
            vec.drain()
            vec.tensor_tensor(out=mk, in0=u, in1=r4, op=ALU.add)
            vec.drain()
            vec.scalar_tensor_tensor(out=r4, in0=mk, scalar=0.5, in1=bb,
                                     op0=ALU.mult, op1=ALU.add)
            vec.drain()
            vec.tensor_tensor(out=bb, in0=r4, in1=cat, op=ALU.mult)
            vec.drain()
            if kind in ("g", "u"):
                vec.tensor_tensor(
                    out=slot[:, 0:1024].rearrange("p (a n) -> p a n", a=2),
                    in0=bb.rearrange("p (a n) -> p a n", a=2),
                    in1=sc.unsqueeze(1).broadcast_to([128, 2, 512]),
                    op=ALU.mult)
            else:
                vec.tensor_tensor(out=slot[:, 0:512], in0=bb, in1=sc, op=ALU.mult)
            vec.drain().then_inc(sem["e_dq"], 1)

        def dq_store(g, uidx):
            kind, d, wt, hh = ALL_UNITS[uidx]
            g.wait_ge(sem["e_dq"], uidx + 1)
            slot = dqo[uidx % 2]
            if kind in ("g", "u"):
                ar = 0 if kind == "g" else 1
                for pi, ksub in ((0, wt), (1, 8 + wt)):
                    dst = gwi_v[d, ar, :, :, ksub * 128:(ksub + 1) * 128]
                    dst = dst.transpose([1, 0, 2])
                    src2 = slot[:, pi * 512:(pi + 1) * 512]
                    src2 = src2.rearrange("w (ch n) -> w ch n", ch=4)
                    g.dma_start(out=dst, in_=src2).then_inc(sem[f"q_gwS{uidx % 2}"], 16)
            else:
                dst = gwi_v[d, 2, wt, :, 512 * hh:512 * hh + 512]
                g.dma_start(out=dst, in_=slot[:, 0:512]).then_inc(sem[f"q_gwS{uidx % 2}"], 16)

        AS = ExitStack()
        ea = AS.enter_context
        scr1 = ea(nc.sbuf_tensor("scr1", [128, D], F32))
        Mrow = ea(nc.sbuf_tensor("Mrow", [1, 256], F32))
        Krow = ea(nc.sbuf_tensor("Krow", [1, 8], BF16))
        KV2 = ea(nc.sbuf_tensor("KV2", [1, 4], F32))
        Mg = ea(nc.sbuf_tensor("Mg", [1, 4], F32))
        Mb = ea(nc.sbuf_tensor("Mb", [128, 1], F32))
        negM = ea(nc.sbuf_tensor("negM", [128, 1], F32))
        ystg = ea(nc.sbuf_tensor("ystg", [128, 2048], BF16))
        bkv_b = ea(nc.sbuf_tensor("bkv_b", [128, 256], F32))
        nscr = ea(nc.sbuf_tensor("nscr", [128, 128], F32))
        kn2 = ea(nc.sbuf_tensor("kn2", [128, NT], F32))
        vn2 = ea(nc.sbuf_tensor("vn2", [128, NT], F32))
        kvcol = ea(nc.sbuf_tensor("kvcol", [128, 2], F32))
        wkv = ea(nc.sbuf_tensor("wkv", [128, 16 * 256], BF16))
        kvm = ea(nc.sbuf_tensor("kvm", [128, NT * 256], BF16))
        p_t = [ea(nc.sbuf_tensor(f"p_t{i}", [128, 1024], BF16)) for i in range(2)]
        rs = ea(nc.sbuf_tensor("rs", [128, 1024], F32))
        aoT = ea(nc.sbuf_tensor("aoT", [128, 1024], BF16))
        WoF = ea(nc.sbuf_tensor("WoF", [128, D], F32))
        WoFb = ea(nc.sbuf_tensor("WoFb", [128, D], BF16))
        XS = ExitStack()
        xT = XS.enter_context(nc.sbuf_tensor("xT", [128, 16 * T], BF16))
        xc = XS.enter_context(nc.sbuf_tensor("xc", [128, D], BF16))

        @block.sync
        def _(sy):
            for t2 in range(NT):
                sy.dma_start(out=stream[:, t2 * D:(t2 + 1) * D],
                             in_=x_in[t2 * 128:(t2 + 1) * 128, :]
                             ).then_inc(sem[f"q_ina{t2}"], 16)
            sy.dma_start(out=bkv_b[:, :],
                         in_=bkv_in.broadcast_to([128, 256])).then_inc(sem["q_inb"], 16)
            sy.dma_start(out=scr1[:, :],
                         in_=bo_in.broadcast_to([128, D])).then_inc(sem["q_inb"], 16)
            if DQ:
                for u in range(6):
                    dq_issue_loads(sy, u)
            if LVL < 2:
                return
            sy.wait_ge(sem["v_M"], 1)
            sy.dma_start(out=m_d.ap().rearrange("one (p c) -> (one p) c", c=2),
                         in_=kvcol[:, :]).then_inc(sem["q_M"], 16)
            sy.wait_ge(sem["q_M"], 16)
            sy.dma_start(out=Mrow[:, :], in_=m_d.ap()).then_inc(sem["q_M"], 16)
            if DQ:
                for u in range(6, 8):
                    dq_issue_loads(sy, u)

        @block.gpsimd
        def _(g):
            g.dma_start(out=wrm_i.ap()[0:1, :], in_=bkv_in[0:1, 0:16]
                        ).then_inc(sem["q_kvs"], 16)
            g.wait_ge(sem["q_kvs"], 16)
            g.collective_compute("AllGather", ALU.bypass,
                                 replica_groups=[list(range(R))],
                                 ins=[wrm_i.ap().opt()],
                                 outs=[wrm_o.ap().opt()]).then_inc(sem["c_M"], 1)
            g.wait_ge(sem["v_one"], 1)
            g.affine_select(out=idm[:, :], in_=ones_t[:, :], pattern=[[1, 128]],
                            compare_op=ALU.is_equal, fill=0.0,
                            channel_multiplier=-1).then_inc(sem["p_id"], 1)
            g.dma_start(out=wkv[:, :].rearrange("p (k c) -> p k c", c=256)[:, :, 0:128],
                        in_=wk_in.rearrange("(k p) d -> p k d", p=128)
                        ).then_inc(sem["q_wkv"], 16)
            g.dma_start(out=wkv[:, :].rearrange("p (k c) -> p k c", c=256)[:, :, 128:256],
                        in_=wv_in.rearrange("(k p) d -> p k d", p=128)
                        ).then_inc(sem["q_wkv"], 16)
            if DQ:
                g.dma_start(out=gse_g.ap()[:, 0:512].rearrange("(b r) c -> b r c", r=16),
                            in_=gsc_in[:, 0:512].unsqueeze(1).broadcast_to([64, 16, 512])
                            ).then_inc(sem["q_gse0"], 16)
                g.dma_start(out=gse_u.ap()[:, 0:512].rearrange("(b r) c -> b r c", r=16),
                            in_=usc_in[:, 0:512].unsqueeze(1).broadcast_to([64, 16, 512])
                            ).then_inc(sem["q_gse0"], 16)
            if DQ:
                g.dma_start(out=gse_d.ap().rearrange("(b r) c -> b r c", r=16),
                            in_=dsc_in.unsqueeze(1).broadcast_to([64, 16, 2048])
                            ).then_inc(sem["q_gse1"], 16)
                g.dma_start(out=gse_g.ap()[:, 512:2048].rearrange("(b r) c -> b r c", r=16),
                            in_=gsc_in[:, 512:2048].unsqueeze(1).broadcast_to([64, 16, 1536])
                            ).then_inc(sem["q_gse2"], 16)
                g.dma_start(out=gse_u.ap()[:, 512:2048].rearrange("(b r) c -> b r c", r=16),
                            in_=usc_in[:, 512:2048].unsqueeze(1).broadcast_to([64, 16, 1536])
                            ).then_inc(sem["q_gse2"], 16)
            if LVL < 2:
                return
            if DQ:
                dq_store(g, 0)
                dq_store(g, 1)
            g.dma_start(out=kvg_i.ap()[:, 256:272],
                        in_=bkv_in[0:1, 0:16].broadcast_to([T, 16])
                        ).then_inc(sem["q_kvs"], 16)
            for tc in range(NT):
                g.wait_ge(sem["e_kv"], tc + 1)
                g.dma_start(out=kvg_i.ap()[tc * 128:(tc + 1) * 128, 0:256],
                            in_=kvm[:, tc * 256:(tc + 1) * 256]
                            ).then_inc(sem["q_kvs"], 16)
            g.wait_ge(sem["v_M"], 2)
            g.wait_ge(sem["q_kvs"], 16 * 10)
            g.dma_start(out=kvg_i.ap()[0:1, 256:257], in_=KV2[0:1, 0:1]
                        ).then_inc(sem["q_kvs"], 16)
            g.wait_ge(sem["q_kvs"], 16 * 11)
            g.collective_compute("AllGather", ALU.bypass,
                                 replica_groups=[list(range(R))],
                                 ins=[kvg_i.ap().opt()],
                                 outs=[kvg_o.ap().opt()]).then_inc(sem["c_kv"], 1)

            def wo_fold(h):
                if h >= 1:
                    g.wait_ge(sem["q_wo"], 32 * h)
                for cb in range(2):
                    g.dma_start(out=WoF[:, cb * 1024:(cb + 1) * 1024],
                                in_=wo_in[128 * h:128 * (h + 1),
                                          cb * 1024:(cb + 1) * 1024],
                                accum_op=(ALU.add if h > 0 else ALU.bypass)
                                ).then_inc(sem["q_wo"], 16)
            if not DQ:
                for h in range(16):
                    wo_fold(h)
            npass = (NPASS if FULL else 1) if DQ else 0
            for d in range(npass):
                for uu in range(UPP):
                    u = d * UPP + uu
                    if u >= 2:
                        dq_store(g, u)
                    if d == 0 and 2 <= uu < 18:
                        wo_fold(uu - 2)
                    nxt = u + 3
                    if nxt >= 8 and nxt < (N_UNITS if FULL else UPP):
                        dq_issue_loads(g, nxt)
                g.wait_ge(sem["q_gwS0"], 16 * SPC[0][(d + 1) * UPP])
                g.wait_ge(sem["q_gwS1"], 16 * SPC[1][(d + 1) * UPP])
                g.collective_compute(
                    "AllGather", ALU.bypass,
                    replica_groups=[list(range(R))],
                    ins=[gw_i.ap()[d * PIECE_ROWS:(d + 1) * PIECE_ROWS, :].opt()],
                    outs=[gw_o.ap()[d * R * PIECE_ROWS:(d + 1) * R * PIECE_ROWS,
                                    :].opt()],
                ).then_inc(c_gw[d], 1)

        @block.tensor
        def _(te):
            te.wait_ge(sem["q_wkv"], 32)
            te.wait_ge(sem["p_id"], 1)
            for tc in range(NT):
                te.wait_ge(sem["a_xc"], tc + 1)
                for b in range(4):
                    m = 4 * tc + b
                    if m >= 4:
                        te.wait_ge(sem["e_xp"], m - 3)
                    for kk in range(4):
                        k = 4 * b + kk
                        mm = te.matmul(out=P[2 + (m % 4)][:, kk * 128:(kk + 1) * 128],
                                       lhsT=xc[:, k * 128:(k + 1) * 128],
                                       rhs=idm[:, :], start=True, stop=True,
                                       skip_group_check=True)
                    mm.then_inc(sem["t_xp"], 1)
                if tc >= 2:
                    te.wait_ge(sem["e_kv"], tc - 1)
                te.wait_ge(sem["e_xp"], 4 * (tc + 1))
                for k in range(16):
                    mm = te.matmul(out=P[tc % 2][:, 0:256],
                                   lhsT=xT[:, k * T + tc * 128:k * T + tc * 128 + 128],
                                   rhs=wkv[:, k * 256:(k + 1) * 256],
                                   start=(k == 0), stop=(k == 15),
                                   skip_group_check=True)
                mm.then_inc(sem["t_kv"], 1)

        XS.close()
        kT = ea(nc.sbuf_tensor("kT", [128, S], BF16))
        vT = ea(nc.sbuf_tensor("vT", [128, T], BF16))
        v_all = ea(nc.sbuf_tensor("v_all", [128, 64 * 128], BF16))

        @block.sync
        def _(sy):
            if LVL < 2:
                return
            sy.wait_ge(sem["c_kv"], 1)
            sy.dma_start_transpose(out=kT[:, :], in_=kvg_o.ap()[:, 0:128]
                                   ).then_inc(sem["q_kT"], 16)
            sy.dma_start_transpose(out=vT[:, :], in_=kvg_i.ap()[:, 128:256]
                                   ).then_inc(sem["q_vT"], 16)
            sy.dma_start(out=Krow[0:1, 0:8],
                         in_=kvg_o.ap().rearrange("(r t) c -> t r c", t=T)[0:1, :, 256],
                         allow_slow_non_contiguous=True).then_inc(sem["q_kr"], 16)
            sy.dma_start(out=v_all[:, :].rearrange("p (ic d) -> p ic d", d=128),
                         in_=kvg_o.ap()[:, 128:256].rearrange("(ic p) d -> p ic d", p=128)
                         ).then_inc(sem["q_va"], 16)
            sy.wait_ge(sem["v_M"], 3)
            sy.dma_start(out=m_d.ap()[:, 2:3], in_=KV2[:, 3:4]).then_inc(sem["q_M"], 16)
            sy.wait_ge(sem["q_M"], 48)
            sy.dma_start(out=Mb[:, :], in_=m_d.ap()[:, 2:3].broadcast_to([128, 1])
                         ).then_inc(sem["q_M"], 16)
            for ti in range(NT if (LVL >= 3 and not FULL) else 0):
                sy.wait_ge(sem["v_ys"], ti + 1)
                sy.dma_start(out=yb_e.ap()[ti * 128:(ti + 1) * 128, :],
                             in_=ystg[:, 0:1024]).then_inc(sem["q_yb"], 16)
                sy.dma_start(out=yb_o.ap()[ti * 128:(ti + 1) * 128, :],
                             in_=ystg[:, 1024:2048]).then_inc(sem["q_yb"], 16)
                sy.wait_ge(sem["q_yb"], 32 * (ti + 1))

        @block.tensor
        def _(te):
            te.wait_ge(sem["q_kT"], 16)
            te.wait_ge(sem["q_vT"], 16)
            te.wait_ge(sem["e_kv"], 8)

            def sT(ic):
                bs = (P[3], P[4]) if ic % 2 == 0 else (P[5], P[6])
                if ic >= 2:
                    te.wait_ge(sem["a_exp"], 2 * (ic - 1))
                for jh in range(2):
                    te.matmul(out=bs[jh][:, :],
                              lhsT=kT[:, ic * 128:(ic + 1) * 128],
                              rhs=vT[:, jh * 512:(jh + 1) * 512],
                              start=True, stop=True, skip_group_check=True
                              ).then_inc(sem["t_sT"], 1)

            def PV(ic):
                if ic == 0:
                    te.wait_ge(sem["q_va"], 16)
                te.wait_ge(sem["a_exp"], 2 * (ic + 1))
                pt = p_t[ic % 2]
                vt = v_all[:, ic * 128:(ic + 1) * 128]
                for jh in range(2):
                    te.matmul(out=P[1 + jh][:, :], lhsT=vt,
                              rhs=pt[:, jh * 512:(jh + 1) * 512],
                              start=(ic == 0), stop=(ic == 63), skip_group_check=True)
                for jh in range(2):
                    mm = te.matmul(out=(P[7] if jh == 0 else P[0])[:, :],
                                   lhsT=ones_t[:, :],
                                   rhs=pt[:, jh * 512:(jh + 1) * 512],
                                   start=(ic == 0), stop=(ic == 63),
                                   skip_group_check=True)
                mm.then_inc(sem["t_pv"], 1)

            sT(0)
            sT(1)
            for ic in range(64):
                PV(ic)
                if ic + 2 < 64:
                    sT(ic + 2)
            if LVL < 3:
                return
            te.wait_ge(sem["v_ao"], 1)
            for jc in range(NT):
                if jc >= 1:
                    te.wait_ge(sem["v_pj"], jc)
                for cb in range(4):
                    mm = te.matmul(out=P[3 + cb][:, :],
                                   lhsT=aoT[:, jc * 128:(jc + 1) * 128],
                                   rhs=WoFb[:, cb * 512:(cb + 1) * 512],
                                   start=True, stop=True, skip_group_check=True)
                mm.then_inc(sem["t_pj"], 1)


        @block.scalar
        def _(sc):
            def kvstat(j):
                sc.wait_ge(sem["e_kv"], j + 1)
                sc.activation(out=nscr[:, :], in_=kvm[:, j * 256:j * 256 + 128],
                              func=AF.Square, accum_out=kn2[:, j:j + 1])
                nc.scalar.drain()
                sc.activation(out=nscr[:, :], in_=kvm[:, j * 256 + 128:j * 256 + 256],
                              func=AF.Square, accum_out=vn2[:, j:j + 1])
                nc.scalar.drain()

            for tc in range(NT):
                sc.wait_ge(sem[f"q_ina{tc}"], 16)
                if tc >= 1:
                    sc.wait_ge(sem["t_xp"], 4 * tc)
                sc.activation(out=xc[:, :], in_=stream[:, tc * D:(tc + 1) * D],
                              func=AF.Copy)
                nc.scalar.drain().then_inc(sem["a_xc"], 1)
                if tc >= 1:
                    kvstat(tc - 1)
            kvstat(NT - 1)
            nc.scalar.drain().then_inc(sem["a_nrm"], 1)
            if LVL < 2:
                return
            sc.wait_ge(sem["q_M"], 64)
            sc.wait_ge(sem["v_M"], 4)
            for ic in range(64):
                bs = (P[3], P[4]) if ic % 2 == 0 else (P[5], P[6])
                if ic >= 2:
                    sc.wait_ge(sem["t_pv"], ic - 1)
                for jh in range(2):
                    sc.wait_ge(sem["t_sT"], 2 * ic + jh + 1)
                    sc.activation(out=p_t[ic % 2][:, jh * 512:(jh + 1) * 512],
                                  in_=bs[jh][:, :], func=AF.Exp,
                                  bias=negM[:, :], scale=1.0 / 128.0
                                  ).then_inc(sem["a_exp"], 1)
            if LVL < 3:
                return
            sc.wait_ge(sem["v_pj"], 8)
            for ti in range(NT):
                sc.activation(out=scr1[:, :], in_=stream[:, ti * D:(ti + 1) * D],
                              func=AF.Copy, accum_out=msum[:, ti:ti + 1])
                nc.scalar.drain()
            for ti in range(NT):
                sc.activation(out=scr1[:, :], in_=stream[:, ti * D:(ti + 1) * D],
                              func=AF.Square, accum_out=sscol[:, ti:ti + 1])
                nc.scalar.drain()
            nc.scalar.drain().then_inc(sem["a_ln"], 1)
            sc.wait_ge(sem["v_ln"], 1)
            sc.activation(out=vtm2[:, :], in_=vtm[:, :], func=AF.Ln, bias=1e-5)
            nc.scalar.drain()
            sc.activation(out=rstd[:, :], in_=vtm2[:, :], func=AF.Exp, scale=-0.5)
            nc.scalar.drain().then_inc(sem["a_ln"], 1)

        @block.vector
        def _(vec):
            vec.memset(ones_t[:, :], 1.0)
            vec.drain().then_inc(sem["v_one"], 1)
            vec.memset(consts[:, 0:1], 1e-5)
            vec.memset(KV2[:, :], 0.0)
            vec.drain()
            vec.wait_ge(sem["q_inb"], 32)
            for tc in range(NT):
                for b in range(4):
                    m = 4 * tc + b
                    vec.wait_ge(sem["t_xp"], m + 1)
                    vec.tensor_copy(
                        out=xT.rearrange("p (k t) -> p k t", t=T)[
                            :, 4 * b:4 * b + 4, tc * 128:(tc + 1) * 128],
                        in_=P[2 + (m % 4)][:, :].rearrange("p (k t) -> p k t", t=128))
                    vec.drain().then_inc(sem["e_xp"], 1)
                vec.wait_ge(sem["t_kv"], tc + 1)
                vec.tensor_tensor(out=kvm[:, tc * 256:(tc + 1) * 256],
                                  in0=P[tc % 2][:, 0:256], in1=bkv_b[:, :],
                                  op=ALU.add)
                vec.drain().then_inc(sem["e_kv"], 1)
            if LVL < 2:
                vec.wait_ge(sem["a_nrm"], 1)
                vec.tensor_copy(out=stream[:, 0:2048], in_=kvm[:, :])
                vec.drain().then_inc(sem["v_ys"], 1)
                return
            if DQ:
                dq_decode(vec, 0)
                dq_decode(vec, 1)
            vec.wait_ge(sem["a_nrm"], 1)
            vec.tensor_reduce(out=kvcol[:, 0:1], in_=kn2[:, :], axis=AX.X, op=ALU.max)
            vec.tensor_reduce(out=kvcol[:, 1:2], in_=vn2[:, :], axis=AX.X, op=ALU.max)
            vec.drain().then_inc(sem["v_M"], 1)
            vec.wait_ge(sem["q_M"], 32)
            vec.tensor_reduce(out=KV2[0:1, 0:1], in_=Mrow[0:1, 0:256:2],
                              axis=AX.X, op=ALU.max)
            vec.tensor_reduce(out=KV2[0:1, 1:2], in_=Mrow[0:1, 1:256:2],
                              axis=AX.X, op=ALU.max)
            vec.drain().then_inc(sem["v_M"], 1)
            if DQ:
                for uu in range(2, UPP):
                    dq_decode(vec, uu)
            vec.wait_ge(sem["q_kr"], 16)
            vec.tensor_reduce(out=KV2[0:1, 2:3], in_=Krow[0:1, :], axis=AX.X,
                              op=ALU.max)
            vec.drain()
            vec.tensor_tensor(out=KV2[0:1, 3:4], in0=KV2[0:1, 2:3],
                              in1=KV2[0:1, 1:2], op=ALU.add)
            vec.drain().then_inc(sem["v_M"], 1)
            vec.wait_ge(sem["q_M"], 64)
            vec.tensor_scalar(out=negM[:, :], in0=Mb[:, :], scalar1=-1.01 / 256.0,
                              scalar2=None, op0=ALU.mult)
            vec.drain().then_inc(sem["v_M"], 1)
            vec.wait_ge(sem["q_wo"], 512)
            vec.tensor_copy(out=WoFb[:, :], in_=WoF[:, :])
            vec.drain()
            vec.wait_ge(sem["t_pv"], 64)
            vec.reciprocal(out=rs[:, 0:512], in_=P[7][:, :])
            vec.reciprocal(out=rs[:, 512:1024], in_=P[0][:, :])
            vec.drain()
            vec.tensor_tensor(out=aoT[:, 0:512], in0=P[1][:, :], in1=rs[:, 0:512],
                              op=ALU.mult)
            vec.tensor_tensor(out=aoT[:, 512:1024], in0=P[2][:, :], in1=rs[:, 512:1024],
                              op=ALU.mult)
            vec.drain().then_inc(sem["v_ao"], 1)
            if LVL < 3:
                vec.tensor_copy(out=stream[:, 0:1024], in_=aoT[:, :])
                vec.tensor_copy(out=stream[:, 2048:3072], in_=rs[:, :])
                vec.drain().then_inc(sem["v_ys"], 1)
                return
            vec.wait_ge(sem["q_inb"], 32)
            for jc in range(NT):
                vec.wait_ge(sem["t_pj"], jc + 1)
                for cb in range(4):
                    vec.tensor_tensor(out=stream[:, jc * D + cb * 512:jc * D + (cb + 1) * 512],
                                      in0=stream[:, jc * D + cb * 512:jc * D + (cb + 1) * 512],
                                      in1=P[3 + cb][:, :], op=ALU.add)
                vec.drain()
                vec.tensor_tensor(out=stream[:, jc * D:(jc + 1) * D],
                                  in0=stream[:, jc * D:(jc + 1) * D],
                                  in1=scr1[:, :], op=ALU.add)
                vec.drain().then_inc(sem["v_pj"], 1)
            vec.wait_ge(sem["a_ln"], 1)
            vec.tensor_scalar(out=nmean[:, :], in0=msum[:, :], scalar1=-1.0 / D,
                              scalar2=None, op0=ALU.mult)
            vec.tensor_scalar(out=vtm2[:, :], in0=sscol[:, :], scalar1=1.0 / D,
                              scalar2=None, op0=ALU.mult)
            vec.drain()
            vec.tensor_tensor(out=vtm[:, :], in0=nmean[:, :], in1=nmean[:, :],
                              op=ALU.mult)
            vec.drain()
            vec.tensor_tensor(out=vtm[:, :], in0=vtm2[:, :], in1=vtm[:, :],
                              op=ALU.subtract)
            vec.drain().then_inc(sem["v_ln"], 1)
            vec.wait_ge(sem["a_ln"], 2)
            for ti in range(NT):
                vec.tensor_scalar(out=stream[:, ti * D:(ti + 1) * D],
                                  in0=stream[:, ti * D:(ti + 1) * D],
                                  scalar1=nmean[:, ti:ti + 1],
                                  scalar2=rstd[:, ti:ti + 1],
                                  op0=ALU.add, op1=ALU.mult)
                vec.drain().then_inc(sem["v_ln"], 1)
                if not FULL:
                    vec.tensor_copy(out=ystg[:, 0:1024],
                                    in_=stream[:, ti * D:(ti + 1) * D:2])
                    vec.tensor_copy(out=ystg[:, 1024:2048],
                                    in_=stream[:, ti * D + 1:(ti + 1) * D:2])
                    vec.drain().then_inc(sem["v_ys"], 1)
                    if ti + 1 < NT:
                        vec.wait_ge(sem["q_yb"], 32 * (ti + 1))

        AS.close()

        FS = ExitStack()
        ef = FS.enter_context
        interT = ef(nc.sbuf_tensor("interT", [128, 16 * T], BF16))
        dwb = ef(nc.sbuf_tensor("dwb", [128, 2 * 8192], BF16))
        pe_pl = ef(nc.sbuf_tensor("pe_pl", [128, 8 * T], BF16))
        po_pl = ef(nc.sbuf_tensor("po_pl", [128, 8 * T], BF16))
        gub = ef(nc.sbuf_tensor("gub", [128, 2 * 4096], BF16))
        gelub = ef(nc.sbuf_tensor("gelub", [128, 1024], BF16))

        if FULL:
            @block.sync
            def _(sy):
                sy.wait_ge(sem["v_ln2"], 9)
                sy.dma_start(out=out_y.rearrange("(t p) d -> p t d", p=128),
                             in_=stream[:, :].rearrange("p (t d) -> p t d", d=D)
                             ).then_inc(sem["q_out"], 16)
                sy.wait_ge(sem["q_out"], 16)

            @block.scalar
            def _(sc):
                def load_gu(c):
                    ph, rr, ch = c // 32, (c % 32) // 4, c % 4
                    par = c % 2
                    sc.wait_ge(c_gw[ph], 1)
                    if c >= 2:
                        sc.wait_ge(sem["t_gu"], 4 * (c - 1))
                    dst = gub[:, par * 4096:par * 4096 + 4096]
                    qw = sem["q_wc0" if c % 2 == 0 else "q_wc1"]
                    sc.dma_start(out=dst[:, 0:2048],
                                 in_=gwo_v[ph, rr, 0, ch]).then_inc(qw, 16)
                    sc.dma_start(out=dst[:, 2048:4096],
                                 in_=gwo_v[ph, rr, 1, ch]).then_inc(qw, 16)

                def load_dw(p, hs):
                    par = (4 * p + hs) % 2
                    sc.wait_ge(c_gw[p // 2], 1)
                    gidx = 4 * p + hs
                    if gidx >= 2:
                        sc.wait_ge(sem["t_dn"], 8 * (gidx - 1))
                    qd = sem["q_dw0" if gidx % 2 == 0 else "q_dw1"]
                    for idx in range(16):
                        c = 16 * p + idx
                        ph, rr, ch = c // 32, (c % 32) // 4, c % 4
                        dst = dwb[:, par * 8192 + idx * 512:par * 8192 + (idx + 1) * 512]
                        sc.dma_start(out=dst,
                                     in_=gwo_v[ph, rr, 2, ch, :, 512 * hs:512 * hs + 512]
                                     ).then_inc(qd, 16)

                for q in range(2 * NT):
                    ti, plq = divmod(q, 2)
                    sc.wait_ge(sem["v_ln"], ti + 2)
                    if q >= 1:
                        sc.wait_ge(sem["t_yp"], 2 * q)
                    sc.activation(out=gelub[:, 0:1024],
                                  in_=stream[:, ti * D + plq:(ti + 1) * D:2],
                                  func=AF.Copy)
                    nc.scalar.drain().then_inc(sem["a_yc"], 1)
                load_gu(0)
                load_gu(1)
                for p in range(NGROUP):
                    for idx in range(16):
                        c = 16 * p + idx
                        for th in range(2):
                            sc.wait_ge(sem["t_gu"], 4 * c + th + 1)
                            if c >= 1:
                                sc.wait_ge(sem["v_it"], 2 * (c - 1) + th + 1)
                            sc.activation(out=gelub[:, th * 512:(th + 1) * 512],
                                          in_=(P[0] if th == 0 else P[6])[:, :],
                                          func=AF.Gelu_apprx_tanh
                                          ).then_inc(sem["a_ge"], 1)
                        if c + 2 < 128:
                            load_gu(c + 2)
                        if idx == 12:
                            load_dw(p, 0)
                        if idx == 13:
                            load_dw(p, 1)
                    load_dw(p, 2)
                    load_dw(p, 3)
                sc.wait_ge(sem["v_da"], 256)
                for ti in range(NT):
                    sc.activation(out=pe_pl[:, 0:2048], in_=stream[:, ti * D:(ti + 1) * D],
                                  func=AF.Copy, accum_out=msum[:, ti:ti + 1])
                    nc.scalar.drain()
                for ti in range(NT):
                    sc.activation(out=pe_pl[:, 0:2048], in_=stream[:, ti * D:(ti + 1) * D],
                                  func=AF.Square, accum_out=sscol[:, ti:ti + 1])
                    nc.scalar.drain()
                nc.scalar.drain().then_inc(sem["a_ln2"], 1)
                sc.wait_ge(sem["v_ln2"], 1)
                sc.activation(out=vtm2[:, :], in_=vtm[:, :], func=AF.Ln, bias=1e-5)
                nc.scalar.drain()
                sc.activation(out=rstd[:, :], in_=vtm2[:, :], func=AF.Exp, scale=-0.5)
                nc.scalar.drain().then_inc(sem["a_ln2"], 1)

            @block.tensor
            def _(te):
                te.wait_ge(sem["v_pj"], 8)
                for q in range(2 * NT):
                    te.wait_ge(sem["a_yc"], q + 1)
                    for h in range(2):
                        m = 2 * q + h
                        if m >= 4:
                            te.wait_ge(sem["e_yp"], m - 3)
                        for kk in range(4):
                            tile = 4 * h + kk
                            mm = te.matmul(out=P[2 + (m % 4)][:, kk * 128:(kk + 1) * 128],
                                           lhsT=gelub[:, tile * 128:(tile + 1) * 128],
                                           rhs=idm[:, :], start=True, stop=True,
                                           skip_group_check=True)
                        mm.then_inc(sem["t_yp"], 1)
                te.wait_ge(sem["e_yp"], 32)
                for p in range(NGROUP):
                    for idx in range(16):
                        c = 16 * p + idx
                        te.wait_ge(sem["q_wc0" if c % 2 == 0 else "q_wc1"], 32 * (c // 2 + 1))
                        par = c % 2
                        for th in range(2):
                            if c >= 1:
                                te.wait_ge(sem["a_ge"], 2 * (c - 1) + th + 1)
                            bank = P[0] if th == 0 else P[6]
                            for k in range(16):
                                pl = pe_pl if k < 8 else po_pl
                                kk = k % 8
                                mm = te.matmul(out=bank[:, :],
                                               lhsT=gub[:, par * 4096 + k * 128:par * 4096 + (k + 1) * 128],
                                               rhs=pl[:, kk * T + th * 512:kk * T + th * 512 + 512],
                                               start=(k == 0), stop=(k == 15),
                                               skip_group_check=True)
                            mm.then_inc(sem["t_gu"], 1)
                        for th in range(2):
                            if c >= 1:
                                te.wait_ge(sem["v_it"], 2 * (c - 1) + th + 1)
                            bank = P[1] if th == 0 else P[7]
                            for k in range(16):
                                pl = pe_pl if k < 8 else po_pl
                                kk = k % 8
                                mm = te.matmul(out=bank[:, :],
                                               lhsT=gub[:, par * 4096 + 2048 + k * 128:par * 4096 + 2048 + (k + 1) * 128],
                                               rhs=pl[:, kk * T + th * 512:kk * T + th * 512 + 512],
                                               start=(k == 0), stop=(k == 15),
                                               skip_group_check=True)
                            mm.then_inc(sem["t_gu"], 1)
                    te.wait_ge(sem["v_it"], 32 * (p + 1))
                    for hs in range(4):
                        gidx = 4 * p + hs
                        par = gidx % 2
                        te.wait_ge(sem["q_dw0" if gidx % 2 == 0 else "q_dw1"], 256 * (gidx // 2 + 1))
                        for tc in range(NT):
                            kk = hs * 8 + tc
                            bank = P[2 + kk % 4]
                            if 32 * p + kk >= 4:
                                te.wait_ge(sem["v_da"], 32 * p + kk - 3)
                            for idx in range(16):
                                mm = te.matmul(out=bank[:, :],
                                               lhsT=interT[:, idx * T + tc * 128:idx * T + tc * 128 + 128],
                                               rhs=dwb[:, par * 8192 + idx * 512:par * 8192 + (idx + 1) * 512],
                                               start=(idx == 0), stop=(idx == 15),
                                               skip_group_check=True)
                            mm.then_inc(sem["t_dn"], 1)

            @block.vector
            def _(vec):
                for m in range(4 * NT):
                    q, h = divmod(m, 2)
                    ti, plq = divmod(q, 2)
                    pl = pe_pl if plq == 0 else po_pl
                    vec.wait_ge(sem["t_yp"], m + 1)
                    vec.tensor_copy(
                        out=pl.rearrange("p (k t) -> p k t", t=T)[
                            :, 4 * h:4 * h + 4, ti * 128:(ti + 1) * 128],
                        in_=P[2 + (m % 4)][:, :].rearrange("p (k t) -> p k t", t=128))
                    vec.drain().then_inc(sem["e_yp"], 1)
                uq = UPP
                for p in range(NGROUP):
                    for idx in range(16):
                        c = 16 * p + idx
                        while uq < N_UNITS and uq <= UPP + 2 * (c + 1) + 4:
                            dq_decode(vec, uq)
                            uq += 1
                        for th in range(2):
                            vec.wait_ge(sem["a_ge"], 2 * c + th + 1)
                            vec.wait_ge(sem["t_gu"], 4 * c + 2 + th + 1)
                            vec.tensor_tensor(
                                out=interT[:, idx * T + th * 512:idx * T + (th + 1) * 512],
                                in0=gelub[:, th * 512:(th + 1) * 512],
                                in1=(P[1] if th == 0 else P[7])[:, :],
                                op=ALU.mult)
                            vec.drain().then_inc(sem["v_it"], 1)
                    for hs in range(4):
                        for tc in range(NT):
                            kk = hs * 8 + tc
                            bank = P[2 + kk % 4]
                            vec.wait_ge(sem["t_dn"], 32 * p + kk + 1)
                            sl = stream[:, tc * D + hs * 512:tc * D + (hs + 1) * 512]
                            vec.tensor_tensor(out=sl, in0=sl, in1=bank[:, :], op=ALU.add)
                            vec.drain().then_inc(sem["v_da"], 1)
                vec.wait_ge(sem["a_ln2"], 1)
                vec.tensor_scalar(out=nmean[:, :], in0=msum[:, :], scalar1=-1.0 / D,
                                  scalar2=None, op0=ALU.mult)
                vec.tensor_scalar(out=vtm2[:, :], in0=sscol[:, :], scalar1=1.0 / D,
                                  scalar2=None, op0=ALU.mult)
                vec.drain()
                vec.tensor_tensor(out=vtm[:, :], in0=nmean[:, :], in1=nmean[:, :],
                                  op=ALU.mult)
                vec.drain()
                vec.tensor_tensor(out=vtm[:, :], in0=vtm2[:, :], in1=vtm[:, :],
                                  op=ALU.subtract)
                vec.drain().then_inc(sem["v_ln2"], 1)
                vec.wait_ge(sem["a_ln2"], 2)
                for ti in range(NT):
                    vec.tensor_scalar(out=stream[:, ti * D:(ti + 1) * D],
                                      in0=stream[:, ti * D:(ti + 1) * D],
                                      scalar1=nmean[:, ti:ti + 1],
                                      scalar2=rstd[:, ti:ti + 1],
                                      op0=ALU.add, op1=ALU.mult)
                    vec.drain().then_inc(sem["v_ln2"], 1)
        else:
            @block.sync
            def _(sy):
                sy.wait_ge(sem["v_ys"], NT if LVL >= 3 else 1)
                sy.dma_start(out=out_y.rearrange("(t p) d -> p t d", p=128),
                             in_=stream[:, :].rearrange("p (t d) -> p t d", d=D)
                             ).then_inc(sem["q_out"], 16)
                sy.wait_ge(sem["q_out"], 16)

        FS.close()
    return nc


_NC_CACHE = {}


def _get_nc(stage):
    if stage not in _NC_CACHE:
        _NC_CACHE[stage] = build_nc(stage)
    return _NC_CACHE[stage]


def _prep_inputs(L0_attn_q, L0_attn_Wk, L0_attn_bk, L0_attn_Wv, L0_attn_bv,
                 L0_attn_Wo, L0_attn_bo,
                 L0_ffn_gate_W_packed, L0_ffn_gate_scales,
                 L0_ffn_up_W_packed, L0_ffn_up_scales,
                 L0_ffn_down_W_packed, L0_ffn_down_scales):
    x = np.ascontiguousarray(np.asarray(L0_attn_q, dtype=np.float32).reshape(S, D))
    wk = np.ascontiguousarray(np.asarray(L0_attn_Wk, dtype=np.float32))
    wv = np.ascontiguousarray(np.asarray(L0_attn_Wv, dtype=np.float32))
    bkv = np.concatenate([np.asarray(L0_attn_bk, dtype=np.float32),
                          np.asarray(L0_attn_bv, dtype=np.float32)]).reshape(1, 256)
    bo = np.asarray(L0_attn_bo, dtype=np.float32).reshape(1, D)
    wo = np.ascontiguousarray(np.asarray(L0_attn_Wo, dtype=np.float32))
    gw = np.asarray(L0_ffn_gate_W_packed, dtype=np.int32)
    gs = np.asarray(L0_ffn_gate_scales, dtype=np.float32)
    uw = np.asarray(L0_ffn_up_W_packed, dtype=np.int32)
    us = np.asarray(L0_ffn_up_scales, dtype=np.float32)
    dw = np.asarray(L0_ffn_down_W_packed, dtype=np.int32)
    ds = np.asarray(L0_ffn_down_scales, dtype=np.float32)

    maps = []
    for r in range(R):
        rows = np.r_[2048 * r:2048 * (r + 1):2, 2048 * r + 1:2048 * (r + 1):2]
        m = dict(
            x_in=np.ascontiguousarray(x[r * T:(r + 1) * T]),
            wk_in=wk, wv_in=wv, bkv_in=bkv, bo_in=bo, wo_in=wo,
            gpk_in=np.ascontiguousarray(gw[rows, :].T),
            gsc_in=np.ascontiguousarray(gs[rows, :].T),
            upk_in=np.ascontiguousarray(uw[rows, :].T),
            usc_in=np.ascontiguousarray(us[rows, :].T),
            dpk_in=np.ascontiguousarray(dw[:, 1024 * r:1024 * (r + 1)].T),
            dsc_in=np.ascontiguousarray(ds[:, 64 * r:64 * (r + 1)].T),
        )
        maps.append(m)
    return maps


def kernel(L0_attn_q, L0_attn_Wq, L0_attn_bq, L0_attn_Wk, L0_attn_bk,
           L0_attn_Wv, L0_attn_bv, L0_attn_Wo, L0_attn_bo,
           L0_ffn_gate_W_packed, L0_ffn_gate_scales,
           L0_ffn_up_W_packed, L0_ffn_up_scales,
           L0_ffn_down_W_packed, L0_ffn_down_scales,
           _stage="full"):
    maps = _prep_inputs(L0_attn_q, L0_attn_Wk, L0_attn_bk, L0_attn_Wv,
                        L0_attn_bv, L0_attn_Wo, L0_attn_bo,
                        L0_ffn_gate_W_packed, L0_ffn_gate_scales,
                        L0_ffn_up_W_packed, L0_ffn_up_scales,
                        L0_ffn_down_W_packed, L0_ffn_down_scales)
    nc = _get_nc(_stage)
    res = run_bass_kernel_spmd(nc, maps, core_ids=list(range(R)))
    out = np.concatenate([res.results[r]["out_y"] for r in range(R)], axis=0)
    return out.reshape(1, S, D).astype(np.float32)


# version 11
# speedup vs baseline: 1.0305x; 1.0305x over previous
import numpy as np
import ml_dtypes
import concourse.bass as bass
import concourse.mybir as mybir
from concourse.bass_utils import run_bass_kernel_spmd

F32 = mybir.dt.float32
BF16 = mybir.dt.bfloat16
I32 = mybir.dt.int32
AF = mybir.ActivationFunctionType
ALU = mybir.AluOpType
AX = mybir.AxisListType

R = 8
T = 1024
S = 8192
D = 2048
DH = 128
NT = 8
NPIECE = 4
NPASS = 4
PIECE_ROWS = 1536
BLK_ROWS = NPIECE * PIECE_ROWS
UPP = 32
NGROUP = 8
STORES_PER_PASS = 48


def build_nc(stage="full"):
    LVL = {"kv": 1, "sm": 2, "sm_nodq": 2, "attn": 3, "attn_nodq": 3,
           "full": 4}[stage]
    DQ = LVL >= 2 and not stage.endswith("nodq")
    FULL = LVL >= 4
    nc = bass.Bass("TRN2", target_bir_lowering=False, debug=False)

    x_in = nc.dram_tensor("x_in", [T, D], F32, kind="ExternalInput").ap()
    wk_in = nc.dram_tensor("wk_in", [D, DH], F32, kind="ExternalInput").ap()
    wv_in = nc.dram_tensor("wv_in", [D, DH], F32, kind="ExternalInput").ap()
    bkv_in = nc.dram_tensor("bkv_in", [1, 256], F32, kind="ExternalInput").ap()
    bo_in = nc.dram_tensor("bo_in", [1, D], F32, kind="ExternalInput").ap()
    wo_in = nc.dram_tensor("wo_in", [D, D], F32, kind="ExternalInput").ap()
    gpk_in = nc.dram_tensor("gpk_in", [1024, D], I32, kind="ExternalInput").ap()
    gsc_in = nc.dram_tensor("gsc_in", [64, D], F32, kind="ExternalInput").ap()
    upk_in = nc.dram_tensor("upk_in", [1024, D], I32, kind="ExternalInput").ap()
    usc_in = nc.dram_tensor("usc_in", [64, D], F32, kind="ExternalInput").ap()
    dpk_in = nc.dram_tensor("dpk_in", [1024, D], I32, kind="ExternalInput").ap()
    dsc_in = nc.dram_tensor("dsc_in", [64, D], F32, kind="ExternalInput").ap()
    out_y = nc.dram_tensor("out_y", [T, D], F32, kind="ExternalOutput").ap()

    kvg_i = nc.dram_tensor("kvg_i", [T, 272], BF16)
    kvg_o = nc.dram_tensor("kvg_o", [S, 272], BF16, addr_space="Shared")
    wrm_i = nc.dram_tensor("wrm_i", [1, 16], BF16)
    wrm_o = nc.dram_tensor("wrm_o", [8, 16], BF16, addr_space="Shared")
    gw_i = nc.dram_tensor("gw_i", [BLK_ROWS, 2048], BF16)
    gw_o = nc.dram_tensor("gw_o", [NPIECE * R * PIECE_ROWS, 2048], BF16, addr_space="Shared")
    gse_g = nc.dram_tensor("gse_g", [1024, 2048], F32)
    gse_u = nc.dram_tensor("gse_u", [1024, 2048], F32)
    gse_d = nc.dram_tensor("gse_d", [1024, 2048], F32)
    yb_e = nc.dram_tensor("yb_e", [T, 1024], BF16)
    yb_o = nc.dram_tensor("yb_o", [T, 1024], BF16)
    m_d = nc.dram_tensor("m_d", [1, 256], F32)

    gwi_v = gw_i.ap().rearrange("(pc ar ch w) c -> pc ar ch w c",
                                ar=3, ch=4, w=128)
    gwo_v = gw_o.ap().rearrange("(pc r ar ch w) c -> pc r ar ch w c",
                                r=R, ar=3, ch=4, w=128)

    from contextlib import ExitStack
    with ExitStack() as ES:
        e = ES.enter_context
        stream = e(nc.sbuf_tensor("stream", [128, NT * D], F32))
        pkb = [e(nc.sbuf_tensor(f"pkb{i}", [128, 512], I32)) for i in range(4)]
        scb = [e(nc.sbuf_tensor(f"scb{i}", [128, 512], F32)) for i in range(4)]
        dsc = {n: e(nc.sbuf_tensor(f"dq_{n}", [128, 1024], BF16))
               for n in ("mk", "u", "r4", "b")}
        dqo = [e(nc.sbuf_tensor(f"dqo{i}", [128, 1024], BF16)) for i in range(2)]
        msum = e(nc.sbuf_tensor("msum", [128, NT], F32))
        sscol = e(nc.sbuf_tensor("sscol", [128, NT], F32))
        nmean = e(nc.sbuf_tensor("nmean", [128, NT], F32))
        rstd = e(nc.sbuf_tensor("rstd", [128, NT], F32))
        vtm = e(nc.sbuf_tensor("vtm", [128, NT], F32))
        vtm2 = e(nc.sbuf_tensor("vtm2", [128, NT], F32))
        consts = e(nc.sbuf_tensor("consts", [128, 2], F32))
        ones_t = e(nc.sbuf_tensor("ones_t", [128, 128], BF16))
        idm = e(nc.sbuf_tensor("idm", [128, 128], BF16))
        P = [e(nc.psum_tensor(f"P{i}", [128, 512], F32)) for i in range(8)]

        names = ["q_ina0", "q_ina1", "q_ina2", "q_ina3", "q_ina4", "q_ina5", "q_ina6", "q_ina7", "q_inb", "q_wkv", "v_one", "p_id", "a_xc", "t_xp", "e_xp", "a_yc", "t_yp", "e_yp", "q_wo", "q_kvs", "q_kT",
                 "q_dqp0", "q_dqp1", "q_dqp2", "q_dqp3", "q_dqg0", "q_dqg1", "q_dqg2", "q_dqg3", "q_gwS0", "q_gwS1", "q_wc0", "q_wc1", "q_dw0", "q_dw1",
                 "q_yb", "q_yT", "q_out", "q_M", "q_gse0", "q_gse1", "q_gse2", "q_vT", "q_kr", "q_va",
                 "c_kv", "c_M",
                 "t_kv", "e_kv", "a_nrm", "v_M", "t_sT", "a_exp", "t_pv",
                 "v_ao", "t_pj", "v_pj", "a_ln", "v_ln", "v_ys",
                 "e_dq", "t_gu", "a_ge", "v_it", "t_dn", "v_da",
                 "a_ln2", "v_ln2"]
        sem = {n: e(nc.semaphore(n)) for n in names}
        c_gw = [e(nc.semaphore(f"c_gw{p}")) for p in range(NPIECE)]
        block = e(nc.Block())

        nc.const_aps.register(F32, 1e-5, consts[:, 0:1])

        def dq_units(d):
            us = [("g", d, wt, 0) for wt in range(8)]
            us += [("u", d, wt, 0) for wt in range(8)]
            us += [("d", d, wt, qh) for wt in range(4) for qh in range(4)]
            return us

        ALL_UNITS = [u for d in range(NPASS) for u in dq_units(d)]
        N_UNITS = len(ALL_UNITS) if FULL else UPP
        SP = [0]
        for _u in ALL_UNITS:
            SP.append(SP[-1] + (2 if _u[0] in ("g", "u") else 1))
        SPC = [[0], [0]]
        for _i, _u in enumerate(ALL_UNITS):
            ns = 2 if _u[0] in ("g", "u") else 1
            SPC[0].append(SPC[0][-1] + (ns if _i % 2 == 0 else 0))
            SPC[1].append(SPC[1][-1] + (ns if _i % 2 == 1 else 0))

        def gse_wait(eng, uidx):
            kind, d, wt, hh = ALL_UNITS[uidx]
            if kind in ("g", "u") and d == 0:
                eng.wait_ge(sem["q_gse0"], 32)
            elif kind == "d":
                eng.wait_ge(sem["q_gse1"], 16)
            else:
                eng.wait_ge(sem["q_gse2"], 32)

        def dq_issue_loads(eng, uidx):
            kind, d, wt, hh = ALL_UNITS[uidx]
            par = uidx % 4
            if uidx < 8:
                qd = sem[f"q_dqp{uidx % 4}"]
            else:
                qd = sem[f"q_dqg{uidx % 4}"]
            if uidx >= 3:
                eng.wait_ge(sem["e_dq"], uidx - 3)
            gse_wait(eng, uidx)
            if kind in ("g", "u"):
                pk_src = gpk_in if kind == "g" else upk_in
                sc_src = gse_g if kind == "g" else gse_u
                eng.dma_start(out=pkb[par][:, 0:512],
                              in_=pk_src[128 * wt:128 * (wt + 1), 512 * d:512 * d + 512]
                              ).then_inc(qd, 16)
                eng.dma_start(out=scb[par][:, 0:512],
                              in_=sc_src.ap()[128 * wt:128 * (wt + 1),
                                              512 * d:512 * d + 512]
                              ).then_inc(qd, 16)
            else:
                w0 = 512 * (d % 2) + 128 * wt
                eng.dma_start(out=pkb[par][:, 0:512],
                              in_=dpk_in[w0:w0 + 128, 512 * hh:512 * hh + 512]
                              ).then_inc(qd, 16)
                eng.dma_start(out=scb[par][:, 0:512],
                              in_=gse_d.ap()[w0:w0 + 128, 512 * hh:512 * hh + 512]
                              ).then_inc(qd, 16)

        def dq_decode(vec, uidx):
            kind, d, wt, hh = ALL_UNITS[uidx]
            par = uidx % 4
            pk = pkb[par][:, 0:512]
            sc = scb[par][:, 0:512]
            W = 1024 if kind in ("g", "u") else 512
            slot = dqo[uidx % 2]
            cat = slot[:, 0:W]
            if uidx < 8:
                vec.wait_ge(sem[f"q_dqp{uidx % 4}"], 32 * (uidx // 4 + 1))
            else:
                vec.wait_ge(sem[f"q_dqg{uidx % 4}"],
                            32 * ((uidx - 8) // 4 + 1))
            if uidx >= 2:
                vec.wait_ge(sem[f"q_gwS{uidx % 2}"], 16 * SPC[uidx % 2][uidx - 1])
            if kind == "d":
                vec.tensor_scalar(out=scb[par][:, 0:512], in0=sc, scalar1=0.5,
                                  scalar2=None, op0=ALU.mult)
                vec.drain()
            if kind in ("g", "u"):
                vec.tensor_single_scalar(out=dsc["u"].bitcast(I32)[:, 0:512], in_=pk, scalar=15,
                                         op=ALU.bitwise_and)
                vec.drain()
                vec.tensor_copy(out=slot[:, 0:512], in_=dsc["u"].bitcast(I32)[:, 0:512])
                vec.drain()
                vec.tensor_single_scalar(out=dsc["u"].bitcast(I32)[:, 0:512], in_=pk, scalar=4,
                                         op=ALU.logical_shift_right)
                vec.drain()
                vec.tensor_copy(out=slot[:, 512:1024], in_=dsc["u"].bitcast(I32)[:, 0:512])
                vec.drain()
            else:
                op = ALU.logical_shift_right if d >= 2 else ALU.bitwise_and
                s0 = 4 if d >= 2 else 15
                vec.tensor_single_scalar(out=dsc["u"].bitcast(I32)[:, 0:512], in_=pk, scalar=s0, op=op)
                vec.drain()
                vec.tensor_copy(out=slot[:, 0:512], in_=dsc["u"].bitcast(I32)[:, 0:512])
                vec.drain()
            mk, u, r4, bb = (dsc["mk"][:, 0:W], dsc["u"][:, 0:W],
                             dsc["r4"][:, 0:W], dsc["b"][:, 0:W])
            vec.tensor_scalar(out=mk, in0=cat, scalar1=7.5, scalar2=None,
                              op0=ALU.is_ge)
            vec.drain()
            vec.scalar_tensor_tensor(out=u, in0=mk, scalar=-8.0, in1=cat,
                                     op0=ALU.mult, op1=ALU.add)
            vec.drain()
            vec.tensor_scalar(out=r4, in0=u, scalar1=-4.0, scalar2=0.0,
                              op0=ALU.add, op1=ALU.max)
            vec.tensor_scalar(out=bb, in0=u, scalar1=-6.0, scalar2=0.0,
                              op0=ALU.add, op1=ALU.max)
            vec.tensor_scalar(out=cat, in0=mk, scalar1=-2.0, scalar2=1.0,
                              op0=ALU.mult, op1=ALU.add)
            vec.drain()
            vec.tensor_tensor(out=mk, in0=u, in1=r4, op=ALU.add)
            vec.drain()
            vec.scalar_tensor_tensor(out=r4, in0=mk, scalar=0.5, in1=bb,
                                     op0=ALU.mult, op1=ALU.add)
            vec.drain()
            vec.tensor_tensor(out=bb, in0=r4, in1=cat, op=ALU.mult)
            vec.drain()
            if kind in ("g", "u"):
                vec.tensor_tensor(
                    out=slot[:, 0:1024].rearrange("p (a n) -> p a n", a=2),
                    in0=bb.rearrange("p (a n) -> p a n", a=2),
                    in1=sc.unsqueeze(1).broadcast_to([128, 2, 512]),
                    op=ALU.mult)
            else:
                vec.tensor_tensor(out=slot[:, 0:512], in0=bb, in1=sc, op=ALU.mult)
            vec.drain().then_inc(sem["e_dq"], 1)

        def dq_store(g, uidx):
            kind, d, wt, hh = ALL_UNITS[uidx]
            g.wait_ge(sem["e_dq"], uidx + 1)
            slot = dqo[uidx % 2]
            if kind in ("g", "u"):
                ar = 0 if kind == "g" else 1
                for pi, ksub in ((0, wt), (1, 8 + wt)):
                    dst = gwi_v[d, ar, :, :, ksub * 128:(ksub + 1) * 128]
                    dst = dst.transpose([1, 0, 2])
                    src2 = slot[:, pi * 512:(pi + 1) * 512]
                    src2 = src2.rearrange("w (ch n) -> w ch n", ch=4)
                    g.dma_start(out=dst, in_=src2).then_inc(sem[f"q_gwS{uidx % 2}"], 16)
            else:
                dst = gwi_v[d, 2, wt, :, 512 * hh:512 * hh + 512]
                g.dma_start(out=dst, in_=slot[:, 0:512]).then_inc(sem[f"q_gwS{uidx % 2}"], 16)

        AS = ExitStack()
        ea = AS.enter_context
        scr1 = ea(nc.sbuf_tensor("scr1", [128, D], F32))
        Mrow = ea(nc.sbuf_tensor("Mrow", [1, 256], F32))
        Krow = ea(nc.sbuf_tensor("Krow", [1, 8], BF16))
        KV2 = ea(nc.sbuf_tensor("KV2", [1, 4], F32))
        Mg = ea(nc.sbuf_tensor("Mg", [1, 4], F32))
        Mb = ea(nc.sbuf_tensor("Mb", [128, 1], F32))
        negM = ea(nc.sbuf_tensor("negM", [128, 1], F32))
        ystg = ea(nc.sbuf_tensor("ystg", [128, 2048], BF16))
        bkv_b = ea(nc.sbuf_tensor("bkv_b", [128, 256], F32))
        nscr = ea(nc.sbuf_tensor("nscr", [128, 128], F32))
        kn2 = ea(nc.sbuf_tensor("kn2", [128, NT], F32))
        vn2 = ea(nc.sbuf_tensor("vn2", [128, NT], F32))
        kvcol = ea(nc.sbuf_tensor("kvcol", [128, 2], F32))
        wkv = ea(nc.sbuf_tensor("wkv", [128, 16 * 256], BF16))
        kvm = ea(nc.sbuf_tensor("kvm", [128, NT * 256], BF16))
        p_t = [ea(nc.sbuf_tensor(f"p_t{i}", [128, 1024], BF16)) for i in range(2)]
        rs = ea(nc.sbuf_tensor("rs", [128, 1024], F32))
        aoT = ea(nc.sbuf_tensor("aoT", [128, 1024], BF16))
        WoF = ea(nc.sbuf_tensor("WoF", [128, D], F32))
        WoFb = ea(nc.sbuf_tensor("WoFb", [128, D], BF16))
        XS = ExitStack()
        xT = XS.enter_context(nc.sbuf_tensor("xT", [128, 16 * T], BF16))
        xc = XS.enter_context(nc.sbuf_tensor("xc", [128, D], BF16))

        @block.sync
        def _(sy):
            for t2 in range(NT):
                sy.dma_start(out=stream[:, t2 * D:(t2 + 1) * D],
                             in_=x_in[t2 * 128:(t2 + 1) * 128, :]
                             ).then_inc(sem[f"q_ina{t2}"], 16)
            sy.dma_start(out=bkv_b[:, :],
                         in_=bkv_in.broadcast_to([128, 256])).then_inc(sem["q_inb"], 16)
            sy.dma_start(out=scr1[:, :],
                         in_=bo_in.broadcast_to([128, D])).then_inc(sem["q_inb"], 16)
            if DQ:
                for u in range(6):
                    dq_issue_loads(sy, u)
            if LVL < 2:
                return
            sy.wait_ge(sem["v_M"], 1)
            sy.dma_start(out=m_d.ap().rearrange("one (p c) -> (one p) c", c=2),
                         in_=kvcol[:, :]).then_inc(sem["q_M"], 16)
            sy.wait_ge(sem["q_M"], 16)
            sy.dma_start(out=Mrow[:, :], in_=m_d.ap()).then_inc(sem["q_M"], 16)
            if DQ:
                for u in range(6, 8):
                    dq_issue_loads(sy, u)

        @block.gpsimd
        def _(g):
            g.dma_start(out=wrm_i.ap()[0:1, :], in_=bkv_in[0:1, 0:16]
                        ).then_inc(sem["q_kvs"], 16)
            g.wait_ge(sem["q_kvs"], 16)
            g.collective_compute("AllGather", ALU.bypass,
                                 replica_groups=[list(range(R))],
                                 ins=[wrm_i.ap().opt()],
                                 outs=[wrm_o.ap().opt()]).then_inc(sem["c_M"], 1)
            g.wait_ge(sem["v_one"], 1)
            g.affine_select(out=idm[:, :], in_=ones_t[:, :], pattern=[[1, 128]],
                            compare_op=ALU.is_equal, fill=0.0,
                            channel_multiplier=-1).then_inc(sem["p_id"], 1)
            g.dma_start(out=wkv[:, :].rearrange("p (k c) -> p k c", c=256)[:, :, 0:128],
                        in_=wk_in.rearrange("(k p) d -> p k d", p=128)
                        ).then_inc(sem["q_wkv"], 16)
            g.dma_start(out=wkv[:, :].rearrange("p (k c) -> p k c", c=256)[:, :, 128:256],
                        in_=wv_in.rearrange("(k p) d -> p k d", p=128)
                        ).then_inc(sem["q_wkv"], 16)
            if DQ:
                g.dma_start(out=gse_g.ap()[:, 0:512].rearrange("(b r) c -> b r c", r=16),
                            in_=gsc_in[:, 0:512].unsqueeze(1).broadcast_to([64, 16, 512])
                            ).then_inc(sem["q_gse0"], 16)
                g.dma_start(out=gse_u.ap()[:, 0:512].rearrange("(b r) c -> b r c", r=16),
                            in_=usc_in[:, 0:512].unsqueeze(1).broadcast_to([64, 16, 512])
                            ).then_inc(sem["q_gse0"], 16)
            if DQ:
                g.dma_start(out=gse_d.ap().rearrange("(b r) c -> b r c", r=16),
                            in_=dsc_in.unsqueeze(1).broadcast_to([64, 16, 2048])
                            ).then_inc(sem["q_gse1"], 16)
                g.dma_start(out=gse_g.ap()[:, 512:2048].rearrange("(b r) c -> b r c", r=16),
                            in_=gsc_in[:, 512:2048].unsqueeze(1).broadcast_to([64, 16, 1536])
                            ).then_inc(sem["q_gse2"], 16)
                g.dma_start(out=gse_u.ap()[:, 512:2048].rearrange("(b r) c -> b r c", r=16),
                            in_=usc_in[:, 512:2048].unsqueeze(1).broadcast_to([64, 16, 1536])
                            ).then_inc(sem["q_gse2"], 16)
            if LVL < 2:
                return
            if DQ:
                dq_store(g, 0)
                dq_store(g, 1)
            g.dma_start(out=kvg_i.ap()[:, 256:272],
                        in_=bkv_in[0:1, 0:16].broadcast_to([T, 16])
                        ).then_inc(sem["q_kvs"], 16)
            for tc in range(NT):
                g.wait_ge(sem["e_kv"], tc + 1)
                g.dma_start(out=kvg_i.ap()[tc * 128:(tc + 1) * 128, 0:256],
                            in_=kvm[:, tc * 256:(tc + 1) * 256]
                            ).then_inc(sem["q_kvs"], 16)
            g.wait_ge(sem["v_M"], 2)
            g.wait_ge(sem["q_kvs"], 16 * 10)
            g.dma_start(out=kvg_i.ap()[0:1, 256:257], in_=KV2[0:1, 0:1]
                        ).then_inc(sem["q_kvs"], 16)
            g.wait_ge(sem["q_kvs"], 16 * 11)
            g.collective_compute("AllGather", ALU.bypass,
                                 replica_groups=[list(range(R))],
                                 ins=[kvg_i.ap().opt()],
                                 outs=[kvg_o.ap().opt()]).then_inc(sem["c_kv"], 1)

            def wo_fold(h):
                if h >= 1:
                    g.wait_ge(sem["q_wo"], 32 * h)
                for cb in range(2):
                    g.dma_start(out=WoF[:, cb * 1024:(cb + 1) * 1024],
                                in_=wo_in[128 * h:128 * (h + 1),
                                          cb * 1024:(cb + 1) * 1024],
                                accum_op=(ALU.add if h > 0 else ALU.bypass)
                                ).then_inc(sem["q_wo"], 16)
            if not DQ:
                for h in range(16):
                    wo_fold(h)
            npass = (NPASS if FULL else 1) if DQ else 0
            for d in range(npass):
                for uu in range(UPP):
                    u = d * UPP + uu
                    if u >= 2:
                        dq_store(g, u)
                    if d == 0 and 2 <= uu < 18:
                        wo_fold(uu - 2)
                    nxt = u + 3
                    if nxt >= 8 and nxt < (N_UNITS if FULL else UPP):
                        dq_issue_loads(g, nxt)
                g.wait_ge(sem["q_gwS0"], 16 * SPC[0][(d + 1) * UPP])
                g.wait_ge(sem["q_gwS1"], 16 * SPC[1][(d + 1) * UPP])
                g.collective_compute(
                    "AllGather", ALU.bypass,
                    replica_groups=[list(range(R))],
                    ins=[gw_i.ap()[d * PIECE_ROWS:(d + 1) * PIECE_ROWS, :].opt()],
                    outs=[gw_o.ap()[d * R * PIECE_ROWS:(d + 1) * R * PIECE_ROWS,
                                    :].opt()],
                ).then_inc(c_gw[d], 1)

        @block.tensor
        def _(te):
            te.wait_ge(sem["q_wkv"], 32)
            te.wait_ge(sem["p_id"], 1)
            for tc in range(NT):
                te.wait_ge(sem["a_xc"], tc + 1)
                for b in range(4):
                    m = 4 * tc + b
                    if m >= 4:
                        te.wait_ge(sem["e_xp"], m - 3)
                    for kk in range(4):
                        k = 4 * b + kk
                        mm = te.matmul(out=P[2 + (m % 4)][:, kk * 128:(kk + 1) * 128],
                                       lhsT=xc[:, k * 128:(k + 1) * 128],
                                       rhs=idm[:, :], start=True, stop=True,
                                       skip_group_check=True)
                    mm.then_inc(sem["t_xp"], 1)
                if tc >= 2:
                    te.wait_ge(sem["e_kv"], tc - 1)
                te.wait_ge(sem["e_xp"], 4 * (tc + 1))
                for k in range(16):
                    mm = te.matmul(out=P[tc % 2][:, 0:256],
                                   lhsT=xT[:, k * T + tc * 128:k * T + tc * 128 + 128],
                                   rhs=wkv[:, k * 256:(k + 1) * 256],
                                   start=(k == 0), stop=(k == 15),
                                   skip_group_check=True)
                mm.then_inc(sem["t_kv"], 1)

        XS.close()
        kT = ea(nc.sbuf_tensor("kT", [128, S], BF16))
        vT = ea(nc.sbuf_tensor("vT", [128, T], BF16))
        v_all = ea(nc.sbuf_tensor("v_all", [128, 64 * 128], BF16))

        @block.sync
        def _(sy):
            if LVL < 2:
                return
            sy.wait_ge(sem["c_kv"], 1)
            sy.dma_start_transpose(out=kT[:, :], in_=kvg_o.ap()[:, 0:128]
                                   ).then_inc(sem["q_kT"], 16)
            sy.dma_start_transpose(out=vT[:, :], in_=kvg_i.ap()[:, 128:256]
                                   ).then_inc(sem["q_vT"], 16)
            sy.dma_start(out=Krow[0:1, 0:8],
                         in_=kvg_o.ap().rearrange("(r t) c -> t r c", t=T)[0:1, :, 256],
                         allow_slow_non_contiguous=True).then_inc(sem["q_kr"], 16)
            sy.dma_start(out=v_all[:, :].rearrange("p (ic d) -> p ic d", d=128),
                         in_=kvg_o.ap()[:, 128:256].rearrange("(ic p) d -> p ic d", p=128)
                         ).then_inc(sem["q_va"], 16)
            sy.wait_ge(sem["v_M"], 3)
            sy.dma_start(out=m_d.ap()[:, 2:3], in_=KV2[:, 3:4]).then_inc(sem["q_M"], 16)
            sy.wait_ge(sem["q_M"], 48)
            sy.dma_start(out=Mb[:, :], in_=m_d.ap()[:, 2:3].broadcast_to([128, 1])
                         ).then_inc(sem["q_M"], 16)
            for ti in range(NT if (LVL >= 3 and not FULL) else 0):
                sy.wait_ge(sem["v_ys"], ti + 1)
                sy.dma_start(out=yb_e.ap()[ti * 128:(ti + 1) * 128, :],
                             in_=ystg[:, 0:1024]).then_inc(sem["q_yb"], 16)
                sy.dma_start(out=yb_o.ap()[ti * 128:(ti + 1) * 128, :],
                             in_=ystg[:, 1024:2048]).then_inc(sem["q_yb"], 16)
                sy.wait_ge(sem["q_yb"], 32 * (ti + 1))

        @block.tensor
        def _(te):
            te.wait_ge(sem["q_kT"], 16)
            te.wait_ge(sem["q_vT"], 16)
            te.wait_ge(sem["e_kv"], 8)

            def sT(ic):
                bs = (P[3], P[4]) if ic % 2 == 0 else (P[5], P[6])
                if ic >= 2:
                    te.wait_ge(sem["a_exp"], 2 * (ic - 1))
                for jh in range(2):
                    te.matmul(out=bs[jh][:, :],
                              lhsT=kT[:, ic * 128:(ic + 1) * 128],
                              rhs=vT[:, jh * 512:(jh + 1) * 512],
                              start=True, stop=True, skip_group_check=True
                              ).then_inc(sem["t_sT"], 1)

            def PV(ic):
                if ic == 0:
                    te.wait_ge(sem["q_va"], 16)
                te.wait_ge(sem["a_exp"], 2 * (ic + 1))
                pt = p_t[ic % 2]
                vt = v_all[:, ic * 128:(ic + 1) * 128]
                for jh in range(2):
                    te.matmul(out=P[1 + jh][:, :], lhsT=vt,
                              rhs=pt[:, jh * 512:(jh + 1) * 512],
                              start=(ic == 0), stop=(ic == 63), skip_group_check=True)
                for jh in range(2):
                    mm = te.matmul(out=(P[7] if jh == 0 else P[0])[:, :],
                                   lhsT=ones_t[:, :],
                                   rhs=pt[:, jh * 512:(jh + 1) * 512],
                                   start=(ic == 0), stop=(ic == 63),
                                   skip_group_check=True)
                mm.then_inc(sem["t_pv"], 1)

            sT(0)
            sT(1)
            for ic in range(64):
                PV(ic)
                if ic + 2 < 64:
                    sT(ic + 2)
            if LVL < 3:
                return
            te.wait_ge(sem["v_ao"], 1)
            for jc in range(NT):
                if jc >= 1:
                    te.wait_ge(sem["v_pj"], jc)
                for cb in range(4):
                    mm = te.matmul(out=P[3 + cb][:, :],
                                   lhsT=aoT[:, jc * 128:(jc + 1) * 128],
                                   rhs=WoFb[:, cb * 512:(cb + 1) * 512],
                                   start=True, stop=True, skip_group_check=True)
                mm.then_inc(sem["t_pj"], 1)


        @block.scalar
        def _(sc):
            def kvstat(j):
                sc.wait_ge(sem["e_kv"], j + 1)
                sc.activation(out=nscr[:, :], in_=kvm[:, j * 256:j * 256 + 128],
                              func=AF.Square, accum_out=kn2[:, j:j + 1])
                nc.scalar.drain()
                sc.activation(out=nscr[:, :], in_=kvm[:, j * 256 + 128:j * 256 + 256],
                              func=AF.Square, accum_out=vn2[:, j:j + 1])
                nc.scalar.drain()

            for tc in range(NT):
                sc.wait_ge(sem[f"q_ina{tc}"], 16)
                if tc >= 1:
                    sc.wait_ge(sem["t_xp"], 4 * tc)
                sc.activation(out=xc[:, :], in_=stream[:, tc * D:(tc + 1) * D],
                              func=AF.Copy)
                nc.scalar.drain().then_inc(sem["a_xc"], 1)
                if tc >= 1:
                    kvstat(tc - 1)
            kvstat(NT - 1)
            nc.scalar.drain().then_inc(sem["a_nrm"], 1)
            if LVL < 2:
                return
            sc.wait_ge(sem["q_M"], 64)
            sc.wait_ge(sem["v_M"], 4)
            for ic in range(64):
                bs = (P[3], P[4]) if ic % 2 == 0 else (P[5], P[6])
                if ic >= 2:
                    sc.wait_ge(sem["t_pv"], ic - 1)
                for jh in range(2):
                    sc.wait_ge(sem["t_sT"], 2 * ic + jh + 1)
                    sc.activation(out=p_t[ic % 2][:, jh * 512:(jh + 1) * 512],
                                  in_=bs[jh][:, :], func=AF.Exp,
                                  bias=negM[:, :], scale=1.0 / 128.0
                                  ).then_inc(sem["a_exp"], 1)
            if LVL < 3:
                return
            sc.wait_ge(sem["v_pj"], 8)
            for ti in range(NT):
                sc.activation(out=scr1[:, :], in_=stream[:, ti * D:(ti + 1) * D],
                              func=AF.Copy, accum_out=msum[:, ti:ti + 1])
                nc.scalar.drain()
            for ti in range(NT):
                sc.activation(out=scr1[:, :], in_=stream[:, ti * D:(ti + 1) * D],
                              func=AF.Square, accum_out=sscol[:, ti:ti + 1])
                nc.scalar.drain()
            nc.scalar.drain().then_inc(sem["a_ln"], 1)
            sc.wait_ge(sem["v_ln"], 1)
            sc.activation(out=vtm2[:, :], in_=vtm[:, :], func=AF.Ln, bias=1e-5)
            nc.scalar.drain()
            sc.activation(out=rstd[:, :], in_=vtm2[:, :], func=AF.Exp, scale=-0.5)
            nc.scalar.drain().then_inc(sem["a_ln"], 1)

        @block.vector
        def _(vec):
            vec.memset(ones_t[:, :], 1.0)
            vec.drain().then_inc(sem["v_one"], 1)
            vec.memset(consts[:, 0:1], 1e-5)
            vec.memset(KV2[:, :], 0.0)
            vec.drain()
            vec.wait_ge(sem["q_inb"], 32)
            for tc in range(NT):
                for b in range(4):
                    m = 4 * tc + b
                    vec.wait_ge(sem["t_xp"], m + 1)
                    vec.tensor_copy(
                        out=xT.rearrange("p (k t) -> p k t", t=T)[
                            :, 4 * b:4 * b + 4, tc * 128:(tc + 1) * 128],
                        in_=P[2 + (m % 4)][:, :].rearrange("p (k t) -> p k t", t=128))
                    vec.drain().then_inc(sem["e_xp"], 1)
                vec.wait_ge(sem["t_kv"], tc + 1)
                vec.tensor_tensor(out=kvm[:, tc * 256:(tc + 1) * 256],
                                  in0=P[tc % 2][:, 0:256], in1=bkv_b[:, :],
                                  op=ALU.add)
                vec.drain().then_inc(sem["e_kv"], 1)
            if LVL < 2:
                vec.wait_ge(sem["a_nrm"], 1)
                vec.tensor_copy(out=stream[:, 0:2048], in_=kvm[:, :])
                vec.drain().then_inc(sem["v_ys"], 1)
                return
            if DQ:
                dq_decode(vec, 0)
                dq_decode(vec, 1)
            vec.wait_ge(sem["a_nrm"], 1)
            vec.tensor_reduce(out=kvcol[:, 0:1], in_=kn2[:, :], axis=AX.X, op=ALU.max)
            vec.tensor_reduce(out=kvcol[:, 1:2], in_=vn2[:, :], axis=AX.X, op=ALU.max)
            vec.drain().then_inc(sem["v_M"], 1)
            vec.wait_ge(sem["q_M"], 32)
            vec.tensor_reduce(out=KV2[0:1, 0:1], in_=Mrow[0:1, 0:256:2],
                              axis=AX.X, op=ALU.max)
            vec.tensor_reduce(out=KV2[0:1, 1:2], in_=Mrow[0:1, 1:256:2],
                              axis=AX.X, op=ALU.max)
            vec.drain().then_inc(sem["v_M"], 1)
            if DQ:
                for uu in range(2, 16):
                    dq_decode(vec, uu)
            vec.wait_ge(sem["q_kr"], 16)
            vec.tensor_reduce(out=KV2[0:1, 2:3], in_=Krow[0:1, :], axis=AX.X,
                              op=ALU.max)
            vec.drain()
            vec.tensor_tensor(out=KV2[0:1, 3:4], in0=KV2[0:1, 2:3],
                              in1=KV2[0:1, 1:2], op=ALU.add)
            vec.drain().then_inc(sem["v_M"], 1)
            vec.wait_ge(sem["q_M"], 64)
            vec.tensor_scalar(out=negM[:, :], in0=Mb[:, :], scalar1=-1.01 / 256.0,
                              scalar2=None, op0=ALU.mult)
            vec.drain().then_inc(sem["v_M"], 1)
            if DQ:
                for uu in range(16, UPP):
                    dq_decode(vec, uu)
            vec.wait_ge(sem["q_wo"], 512)
            vec.tensor_copy(out=WoFb[:, :], in_=WoF[:, :])
            vec.drain()
            vec.wait_ge(sem["t_pv"], 64)
            vec.reciprocal(out=rs[:, 0:512], in_=P[7][:, :])
            vec.reciprocal(out=rs[:, 512:1024], in_=P[0][:, :])
            vec.drain()
            vec.tensor_tensor(out=aoT[:, 0:512], in0=P[1][:, :], in1=rs[:, 0:512],
                              op=ALU.mult)
            vec.tensor_tensor(out=aoT[:, 512:1024], in0=P[2][:, :], in1=rs[:, 512:1024],
                              op=ALU.mult)
            vec.drain().then_inc(sem["v_ao"], 1)
            if LVL < 3:
                vec.tensor_copy(out=stream[:, 0:1024], in_=aoT[:, :])
                vec.tensor_copy(out=stream[:, 2048:3072], in_=rs[:, :])
                vec.drain().then_inc(sem["v_ys"], 1)
                return
            vec.wait_ge(sem["q_inb"], 32)
            for jc in range(NT):
                vec.wait_ge(sem["t_pj"], jc + 1)
                for cb in range(4):
                    vec.tensor_tensor(out=stream[:, jc * D + cb * 512:jc * D + (cb + 1) * 512],
                                      in0=stream[:, jc * D + cb * 512:jc * D + (cb + 1) * 512],
                                      in1=P[3 + cb][:, :], op=ALU.add)
                vec.drain()
                vec.tensor_tensor(out=stream[:, jc * D:(jc + 1) * D],
                                  in0=stream[:, jc * D:(jc + 1) * D],
                                  in1=scr1[:, :], op=ALU.add)
                vec.drain().then_inc(sem["v_pj"], 1)
            vec.wait_ge(sem["a_ln"], 1)
            vec.tensor_scalar(out=nmean[:, :], in0=msum[:, :], scalar1=-1.0 / D,
                              scalar2=None, op0=ALU.mult)
            vec.tensor_scalar(out=vtm2[:, :], in0=sscol[:, :], scalar1=1.0 / D,
                              scalar2=None, op0=ALU.mult)
            vec.drain()
            vec.tensor_tensor(out=vtm[:, :], in0=nmean[:, :], in1=nmean[:, :],
                              op=ALU.mult)
            vec.drain()
            vec.tensor_tensor(out=vtm[:, :], in0=vtm2[:, :], in1=vtm[:, :],
                              op=ALU.subtract)
            vec.drain().then_inc(sem["v_ln"], 1)
            vec.wait_ge(sem["a_ln"], 2)
            for ti in range(NT):
                vec.tensor_scalar(out=stream[:, ti * D:(ti + 1) * D],
                                  in0=stream[:, ti * D:(ti + 1) * D],
                                  scalar1=nmean[:, ti:ti + 1],
                                  scalar2=rstd[:, ti:ti + 1],
                                  op0=ALU.add, op1=ALU.mult)
                vec.drain().then_inc(sem["v_ln"], 1)
                if not FULL:
                    vec.tensor_copy(out=ystg[:, 0:1024],
                                    in_=stream[:, ti * D:(ti + 1) * D:2])
                    vec.tensor_copy(out=ystg[:, 1024:2048],
                                    in_=stream[:, ti * D + 1:(ti + 1) * D:2])
                    vec.drain().then_inc(sem["v_ys"], 1)
                    if ti + 1 < NT:
                        vec.wait_ge(sem["q_yb"], 32 * (ti + 1))

        AS.close()

        FS = ExitStack()
        ef = FS.enter_context
        interT = ef(nc.sbuf_tensor("interT", [128, 16 * T], BF16))
        dwb = ef(nc.sbuf_tensor("dwb", [128, 2 * 8192], BF16))
        pe_pl = ef(nc.sbuf_tensor("pe_pl", [128, 8 * T], BF16))
        po_pl = ef(nc.sbuf_tensor("po_pl", [128, 8 * T], BF16))
        gub = ef(nc.sbuf_tensor("gub", [128, 2 * 4096], BF16))
        gelub = ef(nc.sbuf_tensor("gelub", [128, 1024], BF16))

        if FULL:
            @block.sync
            def _(sy):
                sy.wait_ge(sem["v_ln2"], 9)
                sy.dma_start(out=out_y.rearrange("(t p) d -> p t d", p=128),
                             in_=stream[:, :].rearrange("p (t d) -> p t d", d=D)
                             ).then_inc(sem["q_out"], 16)
                sy.wait_ge(sem["q_out"], 16)

            @block.scalar
            def _(sc):
                def load_gu(c):
                    ph, rr, ch = c // 32, (c % 32) // 4, c % 4
                    par = c % 2
                    sc.wait_ge(c_gw[ph], 1)
                    if c >= 2:
                        sc.wait_ge(sem["t_gu"], 4 * (c - 1))
                    dst = gub[:, par * 4096:par * 4096 + 4096]
                    qw = sem["q_wc0" if c % 2 == 0 else "q_wc1"]
                    sc.dma_start(out=dst[:, 0:2048],
                                 in_=gwo_v[ph, rr, 0, ch]).then_inc(qw, 16)
                    sc.dma_start(out=dst[:, 2048:4096],
                                 in_=gwo_v[ph, rr, 1, ch]).then_inc(qw, 16)

                def load_dw(p, hs):
                    par = (4 * p + hs) % 2
                    sc.wait_ge(c_gw[p // 2], 1)
                    gidx = 4 * p + hs
                    if gidx >= 2:
                        sc.wait_ge(sem["t_dn"], 8 * (gidx - 1))
                    qd = sem["q_dw0" if gidx % 2 == 0 else "q_dw1"]
                    for idx in range(16):
                        c = 16 * p + idx
                        ph, rr, ch = c // 32, (c % 32) // 4, c % 4
                        dst = dwb[:, par * 8192 + idx * 512:par * 8192 + (idx + 1) * 512]
                        sc.dma_start(out=dst,
                                     in_=gwo_v[ph, rr, 2, ch, :, 512 * hs:512 * hs + 512]
                                     ).then_inc(qd, 16)

                for q in range(2 * NT):
                    ti, plq = divmod(q, 2)
                    sc.wait_ge(sem["v_ln"], ti + 2)
                    if q >= 1:
                        sc.wait_ge(sem["t_yp"], 2 * q)
                    sc.activation(out=gelub[:, 0:1024],
                                  in_=stream[:, ti * D + plq:(ti + 1) * D:2],
                                  func=AF.Copy)
                    nc.scalar.drain().then_inc(sem["a_yc"], 1)
                load_gu(0)
                load_gu(1)
                for p in range(NGROUP):
                    for idx in range(16):
                        c = 16 * p + idx
                        for th in range(2):
                            sc.wait_ge(sem["t_gu"], 4 * c + th + 1)
                            if c >= 1:
                                sc.wait_ge(sem["v_it"], 2 * (c - 1) + th + 1)
                            sc.activation(out=gelub[:, th * 512:(th + 1) * 512],
                                          in_=(P[0] if th == 0 else P[6])[:, :],
                                          func=AF.Gelu_apprx_tanh
                                          ).then_inc(sem["a_ge"], 1)
                        if c + 2 < 128:
                            load_gu(c + 2)
                        if idx == 12:
                            load_dw(p, 0)
                        if idx == 13:
                            load_dw(p, 1)
                    load_dw(p, 2)
                    load_dw(p, 3)
                sc.wait_ge(sem["v_da"], 256)
                for ti in range(NT):
                    sc.activation(out=pe_pl[:, 0:2048], in_=stream[:, ti * D:(ti + 1) * D],
                                  func=AF.Copy, accum_out=msum[:, ti:ti + 1])
                    nc.scalar.drain()
                for ti in range(NT):
                    sc.activation(out=pe_pl[:, 0:2048], in_=stream[:, ti * D:(ti + 1) * D],
                                  func=AF.Square, accum_out=sscol[:, ti:ti + 1])
                    nc.scalar.drain()
                nc.scalar.drain().then_inc(sem["a_ln2"], 1)
                sc.wait_ge(sem["v_ln2"], 1)
                sc.activation(out=vtm2[:, :], in_=vtm[:, :], func=AF.Ln, bias=1e-5)
                nc.scalar.drain()
                sc.activation(out=rstd[:, :], in_=vtm2[:, :], func=AF.Exp, scale=-0.5)
                nc.scalar.drain().then_inc(sem["a_ln2"], 1)

            @block.tensor
            def _(te):
                te.wait_ge(sem["v_pj"], 8)
                for q in range(2 * NT):
                    te.wait_ge(sem["a_yc"], q + 1)
                    for h in range(2):
                        m = 2 * q + h
                        if m >= 4:
                            te.wait_ge(sem["e_yp"], m - 3)
                        for kk in range(4):
                            tile = 4 * h + kk
                            mm = te.matmul(out=P[2 + (m % 4)][:, kk * 128:(kk + 1) * 128],
                                           lhsT=gelub[:, tile * 128:(tile + 1) * 128],
                                           rhs=idm[:, :], start=True, stop=True,
                                           skip_group_check=True)
                        mm.then_inc(sem["t_yp"], 1)
                te.wait_ge(sem["e_yp"], 32)
                for p in range(NGROUP):
                    for idx in range(16):
                        c = 16 * p + idx
                        te.wait_ge(sem["q_wc0" if c % 2 == 0 else "q_wc1"], 32 * (c // 2 + 1))
                        par = c % 2
                        for th in range(2):
                            if c >= 1:
                                te.wait_ge(sem["a_ge"], 2 * (c - 1) + th + 1)
                            bank = P[0] if th == 0 else P[6]
                            for k in range(16):
                                pl = pe_pl if k < 8 else po_pl
                                kk = k % 8
                                mm = te.matmul(out=bank[:, :],
                                               lhsT=gub[:, par * 4096 + k * 128:par * 4096 + (k + 1) * 128],
                                               rhs=pl[:, kk * T + th * 512:kk * T + th * 512 + 512],
                                               start=(k == 0), stop=(k == 15),
                                               skip_group_check=True)
                            mm.then_inc(sem["t_gu"], 1)
                        for th in range(2):
                            if c >= 1:
                                te.wait_ge(sem["v_it"], 2 * (c - 1) + th + 1)
                            bank = P[1] if th == 0 else P[7]
                            for k in range(16):
                                pl = pe_pl if k < 8 else po_pl
                                kk = k % 8
                                mm = te.matmul(out=bank[:, :],
                                               lhsT=gub[:, par * 4096 + 2048 + k * 128:par * 4096 + 2048 + (k + 1) * 128],
                                               rhs=pl[:, kk * T + th * 512:kk * T + th * 512 + 512],
                                               start=(k == 0), stop=(k == 15),
                                               skip_group_check=True)
                            mm.then_inc(sem["t_gu"], 1)
                    te.wait_ge(sem["v_it"], 32 * (p + 1))
                    for hs in range(4):
                        gidx = 4 * p + hs
                        par = gidx % 2
                        te.wait_ge(sem["q_dw0" if gidx % 2 == 0 else "q_dw1"], 256 * (gidx // 2 + 1))
                        for tc in range(NT):
                            kk = hs * 8 + tc
                            bank = P[2 + kk % 4]
                            if 32 * p + kk >= 4:
                                te.wait_ge(sem["v_da"], 32 * p + kk - 3)
                            for idx in range(16):
                                mm = te.matmul(out=bank[:, :],
                                               lhsT=interT[:, idx * T + tc * 128:idx * T + tc * 128 + 128],
                                               rhs=dwb[:, par * 8192 + idx * 512:par * 8192 + (idx + 1) * 512],
                                               start=(idx == 0), stop=(idx == 15),
                                               skip_group_check=True)
                            mm.then_inc(sem["t_dn"], 1)

            @block.vector
            def _(vec):
                for m in range(4 * NT):
                    q, h = divmod(m, 2)
                    ti, plq = divmod(q, 2)
                    pl = pe_pl if plq == 0 else po_pl
                    vec.wait_ge(sem["t_yp"], m + 1)
                    vec.tensor_copy(
                        out=pl.rearrange("p (k t) -> p k t", t=T)[
                            :, 4 * h:4 * h + 4, ti * 128:(ti + 1) * 128],
                        in_=P[2 + (m % 4)][:, :].rearrange("p (k t) -> p k t", t=128))
                    vec.drain().then_inc(sem["e_yp"], 1)
                uq = UPP
                for p in range(NGROUP):
                    for idx in range(16):
                        c = 16 * p + idx
                        while uq < N_UNITS and uq <= UPP + 2 * (c + 1) + 4:
                            dq_decode(vec, uq)
                            uq += 1
                        for th in range(2):
                            vec.wait_ge(sem["a_ge"], 2 * c + th + 1)
                            vec.wait_ge(sem["t_gu"], 4 * c + 2 + th + 1)
                            vec.tensor_tensor(
                                out=interT[:, idx * T + th * 512:idx * T + (th + 1) * 512],
                                in0=gelub[:, th * 512:(th + 1) * 512],
                                in1=(P[1] if th == 0 else P[7])[:, :],
                                op=ALU.mult)
                            vec.drain().then_inc(sem["v_it"], 1)
                    for hs in range(4):
                        for tc in range(NT):
                            kk = hs * 8 + tc
                            bank = P[2 + kk % 4]
                            vec.wait_ge(sem["t_dn"], 32 * p + kk + 1)
                            sl = stream[:, tc * D + hs * 512:tc * D + (hs + 1) * 512]
                            vec.tensor_tensor(out=sl, in0=sl, in1=bank[:, :], op=ALU.add)
                            vec.drain().then_inc(sem["v_da"], 1)
                vec.wait_ge(sem["a_ln2"], 1)
                vec.tensor_scalar(out=nmean[:, :], in0=msum[:, :], scalar1=-1.0 / D,
                                  scalar2=None, op0=ALU.mult)
                vec.tensor_scalar(out=vtm2[:, :], in0=sscol[:, :], scalar1=1.0 / D,
                                  scalar2=None, op0=ALU.mult)
                vec.drain()
                vec.tensor_tensor(out=vtm[:, :], in0=nmean[:, :], in1=nmean[:, :],
                                  op=ALU.mult)
                vec.drain()
                vec.tensor_tensor(out=vtm[:, :], in0=vtm2[:, :], in1=vtm[:, :],
                                  op=ALU.subtract)
                vec.drain().then_inc(sem["v_ln2"], 1)
                vec.wait_ge(sem["a_ln2"], 2)
                for ti in range(NT):
                    vec.tensor_scalar(out=stream[:, ti * D:(ti + 1) * D],
                                      in0=stream[:, ti * D:(ti + 1) * D],
                                      scalar1=nmean[:, ti:ti + 1],
                                      scalar2=rstd[:, ti:ti + 1],
                                      op0=ALU.add, op1=ALU.mult)
                    vec.drain().then_inc(sem["v_ln2"], 1)
        else:
            @block.sync
            def _(sy):
                sy.wait_ge(sem["v_ys"], NT if LVL >= 3 else 1)
                sy.dma_start(out=out_y.rearrange("(t p) d -> p t d", p=128),
                             in_=stream[:, :].rearrange("p (t d) -> p t d", d=D)
                             ).then_inc(sem["q_out"], 16)
                sy.wait_ge(sem["q_out"], 16)

        FS.close()
    return nc


_NC_CACHE = {}


def _get_nc(stage):
    if stage not in _NC_CACHE:
        _NC_CACHE[stage] = build_nc(stage)
    return _NC_CACHE[stage]


def _prep_inputs(L0_attn_q, L0_attn_Wk, L0_attn_bk, L0_attn_Wv, L0_attn_bv,
                 L0_attn_Wo, L0_attn_bo,
                 L0_ffn_gate_W_packed, L0_ffn_gate_scales,
                 L0_ffn_up_W_packed, L0_ffn_up_scales,
                 L0_ffn_down_W_packed, L0_ffn_down_scales):
    x = np.ascontiguousarray(np.asarray(L0_attn_q, dtype=np.float32).reshape(S, D))
    wk = np.ascontiguousarray(np.asarray(L0_attn_Wk, dtype=np.float32))
    wv = np.ascontiguousarray(np.asarray(L0_attn_Wv, dtype=np.float32))
    bkv = np.concatenate([np.asarray(L0_attn_bk, dtype=np.float32),
                          np.asarray(L0_attn_bv, dtype=np.float32)]).reshape(1, 256)
    bo = np.asarray(L0_attn_bo, dtype=np.float32).reshape(1, D)
    wo = np.ascontiguousarray(np.asarray(L0_attn_Wo, dtype=np.float32))
    gw = np.asarray(L0_ffn_gate_W_packed, dtype=np.int32)
    gs = np.asarray(L0_ffn_gate_scales, dtype=np.float32)
    uw = np.asarray(L0_ffn_up_W_packed, dtype=np.int32)
    us = np.asarray(L0_ffn_up_scales, dtype=np.float32)
    dw = np.asarray(L0_ffn_down_W_packed, dtype=np.int32)
    ds = np.asarray(L0_ffn_down_scales, dtype=np.float32)

    maps = []
    for r in range(R):
        rows = np.r_[2048 * r:2048 * (r + 1):2, 2048 * r + 1:2048 * (r + 1):2]
        m = dict(
            x_in=np.ascontiguousarray(x[r * T:(r + 1) * T]),
            wk_in=wk, wv_in=wv, bkv_in=bkv, bo_in=bo, wo_in=wo,
            gpk_in=np.ascontiguousarray(gw[rows, :].T),
            gsc_in=np.ascontiguousarray(gs[rows, :].T),
            upk_in=np.ascontiguousarray(uw[rows, :].T),
            usc_in=np.ascontiguousarray(us[rows, :].T),
            dpk_in=np.ascontiguousarray(dw[:, 1024 * r:1024 * (r + 1)].T),
            dsc_in=np.ascontiguousarray(ds[:, 64 * r:64 * (r + 1)].T),
        )
        maps.append(m)
    return maps


def kernel(L0_attn_q, L0_attn_Wq, L0_attn_bq, L0_attn_Wk, L0_attn_bk,
           L0_attn_Wv, L0_attn_bv, L0_attn_Wo, L0_attn_bo,
           L0_ffn_gate_W_packed, L0_ffn_gate_scales,
           L0_ffn_up_W_packed, L0_ffn_up_scales,
           L0_ffn_down_W_packed, L0_ffn_down_scales,
           _stage="full"):
    maps = _prep_inputs(L0_attn_q, L0_attn_Wk, L0_attn_bk, L0_attn_Wv,
                        L0_attn_bv, L0_attn_Wo, L0_attn_bo,
                        L0_ffn_gate_W_packed, L0_ffn_gate_scales,
                        L0_ffn_up_W_packed, L0_ffn_up_scales,
                        L0_ffn_down_W_packed, L0_ffn_down_scales)
    nc = _get_nc(_stage)
    res = run_bass_kernel_spmd(nc, maps, core_ids=list(range(R)))
    out = np.concatenate([res.results[r]["out_y"] for r in range(R)], axis=0)
    return out.reshape(1, S, D).astype(np.float32)
